# version 06
# baseline (speedup 1.0000x reference)
import numpy as np
import concourse.bass as bass
import concourse.mybir as mybir
from concourse.bass_utils import run_bass_kernel_spmd
from concourse.bass_core_rust import IndirectOffsetOnAxis

F32 = mybir.dt.float32
U32 = mybir.dt.uint32
I32 = mybir.dt.int32
AOT = mybir.AluOpType
AF = mybir.ActivationFunctionType

TOK = 2048
OFF = 64
NTOT = TOK + OFF
D = 1024
DG = 8
NST = 16
HEADS = 16
HS = 64
E = 8
CAP = 384
CAPT = CAP // 128
NCOLS = E * CAPT
HID2 = 8192
HID = 4096
NCORES = 8
EPS = 1e-5
POOL = 48896


class Cnt:
    def __init__(self, h):
        self.h = h
        self.c = 0


def build_nc(debug=False, phases=99):
    nc = bass.Bass("TRN2", target_bir_lowering=False)
    ap = {}
    th = {}

    def din(name, shape, dtype=F32):
        th[name] = nc.dram_tensor(name, shape, dtype, kind="ExternalInput")
        ap[name] = th[name].ap()

    def dout(name, shape, dtype=F32):
        th[name] = nc.dram_tensor(name, shape, dtype, kind="ExternalOutput")
        ap[name] = th[name].ap()

    def dint(name, shape, dtype=F32):
        th[name] = nc.dram_tensor(name, shape, dtype)
        ap[name] = th[name].ap()

    din("xT", [D, NTOT])
    din("ident", [128, 128])
    din("eyerep", [16, 16 * 128])
    din("alog_r", [128, DG * NST])
    din("wdt_r", [128, DG]); din("bdt_r", [128, DG])
    din("mur_r", [128, DG]); din("muk_r", [128, DG])
    din("wxpT", [D, 2 * NST]); din("bxp", [NST, 2])
    din("wouts", [DG, 128, 8, 128]); din("bout_r", [128, DG])
    din("wrks", [DG, 128, 16, 128]); din("wvs", [DG, 128, 8, 128])
    din("wqks", [16, 128, 8, 128]); din("wqkv", [D, 3 * D])
    din("bqkv_r", [128, 16]); din("bv_row", [1, D])
    din("wo", [D, D]); din("bo_r", [128, DG])
    din("routw", [D, E]); din("routb", [E, 1])
    din("w1c", [E, 32, 128, 16, 128]); din("b1_r", [E, 128, 64])
    din("w2", [E, HID, D]); din("b2_r", [E, 128, DG])
    din("selp", [128, NCORES])
    din("kidx", [128, 32], U32)
    din("vqidx", [128, 32], U32)

    dout("out", [TOK, D])

    dint("xres", [D, NTOT])
    dint("ybuf", [D, TOK])
    dint("kag_in", [2 * D, D])
    dint("kag_out", [NCORES * 2 * D, D])
    dint("vag_in", [TOK, D])
    dint("vag_out", [NCORES * TOK, D])
    dint("stag_in", [128, 128])
    dint("stag_out", [NCORES * 128, 128])
    dint("xn1tok", [TOK + 128, D])
    dint("moeout", [TOK + 128, D])
    dint("ilist", [128 * NCOLS, 1], U32)

    if debug:
        dout("dbg_xn", [D, NTOT])
        dout("dbg_dt", [D, TOK])
        dout("dbg_bc", [32, TOK])
        dout("dbg_y", [D, TOK])
        dout("dbg_x1", [D, NTOT])
        dout("dbg_xn1", [D, TOK])
        dout("dbg_eid", [128, 16])
        dout("dbg_ilist", [128, NCOLS], U32)
        dout("dbg_q", [D, TOK])
        dout("dbg_moe", [TOK + 128, D])
        dout("dbg_att", [D, NTOT])
        dout("dbg_av", [D, TOK])
        dout("dbg_oacc", [2 * 65, TOK])

    import contextlib
    st = contextlib.ExitStack()
    with st:
        block = st.enter_context(nc.Block())
        T = block.tensor; V = block.vector; A = block.scalar
        G = block.gpsimd; S = block.sync

        LQ = Cnt(st.enter_context(nc.semaphore("LQ")))
        GQ = Cnt(st.enter_context(nc.semaphore("GQ")))
        AQ = Cnt(st.enter_context(nc.semaphore("AQ")))
        TS = Cnt(st.enter_context(nc.semaphore("TS")))
        VS = Cnt(st.enter_context(nc.semaphore("VS")))
        AS = Cnt(st.enter_context(nc.semaphore("AS")))
        GS = Cnt(st.enter_context(nc.semaphore("GS")))
        CC = Cnt(st.enter_context(nc.semaphore("CC")))
        CQ = Cnt(st.enter_context(nc.semaphore("CQ")))
        DSL = [Cnt(st.enter_context(nc.semaphore(f"D{i}"))) for i in range(4)]
        SSL = [Cnt(st.enter_context(nc.semaphore(f"S{i}"))) for i in range(2)]
        GP = [Cnt(st.enter_context(nc.semaphore(f"GP{i}"))) for i in range(2)]
        HS = [Cnt(st.enter_context(nc.semaphore(f"H{i}"))) for i in range(2)]
        GD = [Cnt(st.enter_context(nc.semaphore(f"G{i}"))) for i in range(4)]
        SCT = Cnt(st.enter_context(nc.semaphore("SCT")))
        FIN = Cnt(st.enter_context(nc.semaphore("FIN")))

        pool = st.enter_context(nc.sbuf_tensor("pool", [128, POOL], F32))
        pp = st.enter_context(nc.psum_tensor("pp", [128, 4096], F32))

        ident = st.enter_context(nc.sbuf_tensor("ident_sb", [128, 128], F32))
        eyerep = st.enter_context(nc.sbuf_tensor("eyerep_sb", [16, 16 * 128], F32))
        ones1 = st.enter_context(nc.sbuf_tensor("ones1_sb", [1, 128], F32))
        onesc = st.enter_context(nc.sbuf_tensor("onesc_sb", [128, 1], F32))
        consts = st.enter_context(nc.sbuf_tensor("consts_sb", [128, 80], F32))
        alog = st.enter_context(nc.sbuf_tensor("alog_sb", [128, DG * NST], F32))
        acoef = st.enter_context(nc.sbuf_tensor("acoef_sb", [128, DG * NST], F32))
        routb = st.enter_context(nc.sbuf_tensor("routb_sb", [8, 1], F32))
        bxp = st.enter_context(nc.sbuf_tensor("bxp_sb", [16, 2], F32))

        CWDT, CBDT, CMUR, CMUK, CBOUT, CBO, CSELP = 0, 8, 16, 24, 32, 40, 48
        CBQKV = 56
        CEPS, CNEG1 = 72, 73

        def bank(b, w=512, p0=0, p1=128, off=0):
            return pp[p0:p1, 512 * b + off:512 * b + off + w]

        def XN(g): return pool[:, 2112 * g:2112 * (g + 1)]
        def XN1(g): return pool[:, 2048 * g:2048 * (g + 1)]
        def DT(g): return pool[:, 16896 + 2048 * g:16896 + 2048 * (g + 1)]
        QT = DT
        def XR(g): return pool[:, 16896 + 512 * g:16896 + 512 * (g + 1)]
        def XK(g): return pool[:, 20992 + 512 * g:20992 + 512 * (g + 1)]
        def RS(g): return pool[:, 25088 + 512 * g:25088 + 512 * (g + 1)]
        def KA(g): return pool[:, 29184 + 512 * g:29184 + 512 * (g + 1)]
        def WRK(i, bf): return pool[:, 33280 + 1024 * (2 * bf + i):
                                    33280 + 1024 * (2 * bf + i) + 1024]
        def WV(bf): return pool[:, 37376 + 1024 * bf:37376 + 1024 * bf + 1024]
        wxp_sb = pool[:, 33280:33536]
        X0 = pool[0:1, 34304:36352]
        def YC(g): return pool[:, 33280 + 512 * g:33280 + 512 * (g + 1)]
        YG = pool[:, 33280:35328]
        def DA(i): return pool[:, 37376 + 512 * i:37376 + 512 * (i + 1)]
        UU = pool[:, 38400:38912]
        SSb = pool[:, 38912:39424]
        def PR(i): return pool[:, 39424 + 512 * i:39424 + 512 * (i + 1)]
        def SHB(i): return pool[:, 37376 + 512 * i:37376 + 512 * (i + 1)]
        def EV(i): return pool[:, 37376 + 512 * i:37376 + 512 * (i + 1)]
        SCOL = pool[:, 40448:40576]
        SPART = pool[:, 40576:40704]
        BT = pool[0:16, 40704:42752]
        CT = pool[0:16, 42752:44800]
        ZBF = pool[:, 44800:46848]
        A1 = pool[:, 46848:48896]
        def LNS(i): return pool[:, 39936 + 512 * i:39936 + 512 * (i + 1)]
        def WOUT(g): return pool[:, 39936 + 1024 * g:39936 + 1024 * (g + 1)]
        SR_MEAN = pool[0:1, 33280:35392]
        SR_RSTD = pool[0:1, 35392:37504]
        SR_TMP = pool[0:1, 37504:39616]

        @S
        def _(s):
            for nm, dst in [("ident", ident[:]), ("eyerep", eyerep[:]),
                            ("alog_r", alog[:]), ("routb", routb[:]),
                            ("bxp", bxp[:])]:
                s.dma_start(out=dst, in_=ap[nm]).then_inc(CQ.h, 16)
                CQ.c += 16
            for nm, col, w in [("wdt_r", CWDT, 8), ("bdt_r", CBDT, 8),
                               ("mur_r", CMUR, 8), ("muk_r", CMUK, 8),
                               ("bout_r", CBOUT, 8), ("bo_r", CBO, 8),
                               ("selp", CSELP, 8), ("bqkv_r", CBQKV, 16)]:
                s.dma_start(out=consts[:, col:col + w], in_=ap[nm]).then_inc(CQ.h, 16)
                CQ.c += 16
            s.dma_start(out=ap["xres"], in_=ap["xT"]).then_inc(CQ.h, 16)
            CQ.c += 16
        ld_const = CQ.c

        @G
        def _(g):
            g.memset(ones1[:], 1.0)
            g.memset(onesc[:], 1.0)
            g.memset(consts[:, CEPS:CEPS + 1], EPS)
            g.memset(consts[:, CNEG1:CNEG1 + 1], -1.0).then_inc(GS.h, 1)
        GS.c += 1

        @A
        def _(a):
            a.wait_ge(CQ.h, ld_const)
            a.activation(acoef[:], alog[:], AF.Exp).then_inc(AS.h, 1)
        AS.c += 1

        @V
        def _(v, snap=AS.c):
            v.wait_ge(AS.h, snap)
            v.tensor_scalar_mul(acoef[:], acoef[:], -1.0).then_inc(VS.h, 1)
        VS.c += 1

        def emit_layernorm(src_dram, W, dst, pre_wait=None, t_pre=None):
            nch = (W + 511) // 512

            def cols(c):
                lo = c * 512
                return lo, min(lo + 512, W)

            as_base = AS.c
            ts_base = TS.c
            ldslot = [DSL[0].c, DSL[1].c]
            evac = []
            for c in range(nch):
                lo, hi = cols(c)
                w = hi - lo

                @S
                def _(s):
                    if c == 0 and pre_wait is not None:
                        for sem_, val_ in pre_wait:
                            s.wait_ge(sem_.h, val_)
                    for g in range(DG):
                        k = c * DG + g
                        if k >= 2:
                            s.wait_ge(AS.h, as_base + (k - 1))
                            s.wait_ge(TS.h, ts_base + 2 * (k - 1))
                        s.dma_start(out=LNS(k % 2)[:, 0:w],
                                    in_=src_dram[128 * g:128 * (g + 1), lo:hi]
                                    ).then_inc(DSL[k % 2].h, 16)
                        DSL[k % 2].c += 16

                @A
                def _(a):
                    for g in range(DG):
                        k = c * DG + g
                        ldslot[k % 2] += 16
                        a.wait_ge(DSL[k % 2].h, ldslot[k % 2])
                        if k >= 2:
                            a.wait_ge(TS.h, ts_base + 2 * (k - 1))
                        a.activation(LNS(2 + k % 2)[:, 0:w], LNS(k % 2)[:, 0:w],
                                     AF.Square).then_inc(AS.h, 1)
                        AS.c += 1

                @T
                def _(t):
                    if c == 0:
                        t.wait_ge(GS.h, 1)
                        if t_pre is not None:
                            for sem_, val_ in t_pre:
                                t.wait_ge(sem_.h, val_)
                    if c >= 1:
                        t.wait_ge(VS.h, evac[c - 1])
                    for g in range(DG):
                        k = c * DG + g
                        t.wait_ge(AS.h, as_base + k + 1)
                        t.matmul(bank(0, w, 0, 1), onesc[:], LNS(k % 2)[:, 0:w],
                                 start=(g == 0), stop=(g == DG - 1))
                        t.matmul(bank(1, w, 0, 1), onesc[:], LNS(2 + k % 2)[:, 0:w],
                                 start=(g == 0), stop=(g == DG - 1)).then_inc(TS.h, 2)
                        TS.c += 2
                tsnap = TS.c

                @V
                def _(v):
                    v.wait_ge(TS.h, tsnap)
                    v.tensor_scalar_mul(SR_MEAN[:, lo:hi], bank(0, w, 0, 1), 1.0 / D)
                    v.tensor_scalar_mul(SR_TMP[:, lo:hi], bank(1, w, 0, 1), 1.0 / D
                                        ).then_inc(VS.h, 1)
                VS.c += 1
                evac.append(VS.c)

            vsnap = VS.c

            @V
            def _(v):
                v.wait_ge(VS.h, vsnap)
                v.tensor_tensor(SR_RSTD[:, 0:W], SR_MEAN[:, 0:W], SR_MEAN[:, 0:W],
                                AOT.mult)
                v.drain()
                v.tensor_tensor(SR_TMP[:, 0:W], SR_TMP[:, 0:W], SR_RSTD[:, 0:W],
                                AOT.subtract).then_inc(VS.h, 1)
            VS.c += 1
            v2 = VS.c

            @A
            def _(a):
                a.wait_ge(VS.h, v2)
                a.wait_ge(GS.h, 1)
                a.activation(SR_RSTD[:, 0:W], SR_TMP[:, 0:W], AF.Sqrt,
                             bias=consts[0:1, CEPS:CEPS + 1]).then_inc(AS.h, 1)
            AS.c += 1
            a2 = AS.c

            @V
            def _(v):
                v.wait_ge(AS.h, a2)
                v.reciprocal(SR_RSTD[:, 0:W], SR_RSTD[:, 0:W]).then_inc(VS.h, 1)
            VS.c += 1
            rows_ready = VS.c

            vn_base = VS.c
            ld2slot = [DSL[0].c, DSL[1].c]
            for c in range(nch):
                lo, hi = cols(c)
                w = hi - lo

                @T
                def _(t):
                    t.wait_ge(VS.h, rows_ready)
                    if c >= 1:
                        t.wait_ge(VS.h, vn_base + c * DG)
                    t.matmul(bank(2, w), ones1[:], SR_MEAN[:, lo:hi],
                             start=True, stop=True)
                    t.matmul(bank(3, w), ones1[:], SR_RSTD[:, lo:hi],
                             start=True, stop=True).then_inc(TS.h, 1)
                    TS.c += 1
                tsnap2 = TS.c

                @S
                def _(s):
                    for g in range(DG):
                        k = c * DG + g
                        if k < 2:
                            s.wait_ge(VS.h, rows_ready)
                        if k >= 2:
                            s.wait_ge(VS.h, vn_base + (k - 1))
                        s.dma_start(out=LNS(k % 2)[:, 0:w],
                                    in_=src_dram[128 * g:128 * (g + 1), lo:hi]
                                    ).then_inc(DSL[k % 2].h, 16)
                        DSL[k % 2].c += 16

                @V
                def _(v):
                    v.wait_ge(TS.h, tsnap2)
                    for g in range(DG):
                        k = c * DG + g
                        ld2slot[k % 2] += 16
                        v.wait_ge(DSL[k % 2].h, ld2slot[k % 2])
                        v.tensor_tensor(dst(g)[:, lo:hi], LNS(k % 2)[:, 0:w],
                                        bank(2, w), AOT.subtract)
                        v.drain()
                        v.tensor_tensor(dst(g)[:, lo:hi], dst(g)[:, lo:hi],
                                        bank(3, w), AOT.mult).then_inc(VS.h, 1)
                        VS.c += 1

        emit_layernorm(ap["xT"], NTOT, XN)
        xn_done = VS.c

        if debug:
            @S
            def _(s, snap=VS.c):
                s.wait_ge(VS.h, snap)
                for g in range(DG):
                    s.dma_start(out=ap["dbg_xn"][128 * g:128 * (g + 1), :],
                                in_=XN(g)[:]).then_inc(LQ.h, 16)
                    LQ.c += 16
        if phases <= 1:
            _fin(S, G, LQ, GQ, AQ, GS, CQ, DSL, SSL, FIN, HS, GD, SCT)
            return nc

        mix_done = []
        proj_done = []
        rk_mm = []
        rk_ev = []
        kv_ev = []
        kv_st = []
        for tc in range(4):
            lo = OFF + 512 * tc
            hi = lo + 512

            @V
            def _(v):
                v.wait_ge(VS.h, xn_done)
                if tc == 0:
                    v.wait_ge(CQ.h, ld_const)
                if tc >= 1:
                    v.wait_ge(TS.h, proj_done[tc - 1])
                for g in range(DG):
                    v.tensor_tensor(XR(g)[:], XN(g)[:, lo:hi],
                                    XN(g)[:, lo - 1:hi - 1], AOT.subtract)
                    v.drain()
                    v.scalar_tensor_tensor(XK(g)[:], XR(g)[:],
                                           consts[:, CMUK + g:CMUK + g + 1],
                                           XN(g)[:, lo - 1:hi - 1], AOT.mult, AOT.add)
                    v.drain()
                    v.scalar_tensor_tensor(XR(g)[:], XR(g)[:],
                                           consts[:, CMUR + g:CMUR + g + 1],
                                           XN(g)[:, lo - 1:hi - 1], AOT.mult, AOT.add
                                           ).then_inc(VS.h, 1)
                    VS.c += 1
            mix_done.append(VS.c)
            rk_ts_base = TS.c
            wrk_slot = [DSL[2].c, DSL[3].c]

            @S
            def _(s):
                for o in range(DG):
                    ko = tc * DG + o
                    if o >= 2:
                        s.wait_ge(TS.h, rk_ts_base + o - 1)
                    elif ko >= 2:
                        s.wait_ge(TS.h, rk_mm[ko - 2])
                    s.dma_start(
                        out=WRK(0, ko % 2).rearrange("p (g c) -> p g c", g=8),
                        in_=ap["wrks"][o][:, 0:8, :]).then_inc(DSL[2 + ko % 2].h, 16)
                    s.dma_start(
                        out=WRK(1, ko % 2).rearrange("p (g c) -> p g c", g=8),
                        in_=ap["wrks"][o][:, 8:16, :]).then_inc(DSL[2 + ko % 2].h, 16)
                    DSL[2 + ko % 2].c += 32

            if tc == 0:
                kv_mm = []

            for o in range(DG):
                ko = tc * DG + o

                @T
                def _(t):
                    wrk_slot[ko % 2] += 32
                    t.wait_ge(DSL[2 + ko % 2].h, wrk_slot[ko % 2])
                    t.wait_ge(VS.h, mix_done[tc])
                    if ko >= 2:
                        t.wait_ge(AS.h, rk_ev[ko - 2])
                    for g in range(DG):
                        t.matmul(bank(0 + 2 * (ko % 2), 512),
                                 WRK(0, ko % 2)[:, 128 * g:128 * (g + 1)],
                                 XR(g)[:], start=(g == 0), stop=(g == DG - 1))
                        mm = t.matmul(bank(1 + 2 * (ko % 2), 512),
                                      WRK(1, ko % 2)[:, 128 * g:128 * (g + 1)],
                                      XK(g)[:], start=(g == 0), stop=(g == DG - 1))
                    mm.then_inc(TS.h, 1)
                TS.c += 1
                rk_mm.append(TS.c)

                @A
                def _(a):
                    a.wait_ge(TS.h, rk_mm[ko])
                    if ko >= DG:
                        a.wait_ge(VS.h, kv_ev[ko - DG])
                    a.activation(RS(o)[:], bank(0 + 2 * (ko % 2), 512), AF.Sigmoid)
                    a.activation(KA(o)[:], bank(1 + 2 * (ko % 2), 512), AF.Relu
                                 ).then_inc(AS.h, 1)
                AS.c += 1
                rk_ev.append(AS.c)

            @V
            def _(v, snap=AS.c):
                v.wait_ge(AS.h, snap)
                for o in range(DG):
                    v.tensor_tensor(KA(o)[:], KA(o)[:], KA(o)[:], AOT.mult
                                    ).then_inc(VS.h, 1)
                    VS.c += 1
            ka_done = VS.c
            kv_ts_base = TS.c
            wv_slot = [DSL[0].c, DSL[1].c]

            @S
            def _(s):
                for o in range(DG):
                    ko = tc * DG + o
                    if o >= 2:
                        s.wait_ge(TS.h, kv_ts_base + o - 1)
                    elif ko >= 2:
                        s.wait_ge(TS.h, kv_mm[ko - 2])
                    s.dma_start(out=WV(ko % 2).rearrange("p (g c) -> p g c", g=8),
                                in_=ap["wvs"][o]).then_inc(DSL[ko % 2].h, 16)
                    DSL[ko % 2].c += 16

            for o in range(DG):
                ko = tc * DG + o

                @T
                def _(t):
                    wv_slot[ko % 2] += 16
                    t.wait_ge(DSL[ko % 2].h, wv_slot[ko % 2])
                    t.wait_ge(VS.h, ka_done)
                    if ko >= 2:
                        t.wait_ge(VS.h, kv_ev[ko - 2])
                    for g in range(DG):
                        mm = t.matmul(bank(4 + ko % 2, 512),
                                      WV(ko % 2)[:, 128 * g:128 * (g + 1)],
                                      KA(g)[:], start=(g == 0), stop=(g == DG - 1))
                    mm.then_inc(TS.h, 1)
                TS.c += 1
                kv_mm.append(TS.c)
                kvsnap = TS.c

                @V
                def _(v):
                    v.wait_ge(TS.h, kvsnap)
                    if ko >= 2:
                        v.wait_ge(SSL[ko % 2].h, SSL[ko % 2].c)
                    v.tensor_tensor(PR(ko % 2), bank(4 + ko % 2, 512), RS(o)[:],
                                    AOT.mult).then_inc(VS.h, 1)
                VS.c += 1
                kv_ev.append(VS.c)

                @G
                def _(g_):
                    g_.wait_ge(VS.h, kv_ev[ko])
                    g_.dma_start(out=ap["xres"][128 * o:128 * (o + 1), lo:hi],
                                 in_=PR(ko % 2), accum_op=AOT.add
                                 ).then_inc(SSL[ko % 2].h, 16)
                SSL[ko % 2].c += 16
                kv_st.append(None)
            proj_done.append(TS.c)
        ffn_gq = GQ.c

        if phases <= 2:
            _fin(S, G, LQ, GQ, AQ, GS, CQ, DSL, SSL, FIN, HS, GD, SCT)
            return nc

        @S
        def _(s, tsnap=TS.c):
            s.wait_ge(TS.h, tsnap)
            s.dma_start(out=wxp_sb.rearrange("p (g c) -> p g c", g=8),
                        in_=ap["wxpT"].rearrange("(g p) c -> p g c", p=128)
                        ).then_inc(LQ.h, 16)
            LQ.c += 16
        ldw_done = LQ.c

        @V
        def _(v, snap=VS.c):
            v.wait_ge(VS.h, snap)
            v.wait_ge(LQ.h, ldw_done)
            v.wait_ge(CQ.h, ld_const)
            v.tensor_copy(X0[:, 0:TOK], XN(0)[0:1, OFF:]).then_inc(VS.h, 1)
        VS.c += 1
        x0_done = VS.c

        bc_mm = []
        bc_ev = []
        for c in range(4):
            @T
            def _(t):
                if c == 0:
                    t.wait_ge(LQ.h, ldw_done)
                    t.wait_ge(AS.h, AS.c)
                if c >= 2:
                    t.wait_ge(AS.h, bc_ev[c - 2])
                for g in range(DG):
                    t.matmul(bank(0 + c % 2, 512, 0, 16),
                             wxp_sb[:, 32 * g:32 * g + 16],
                             XN(g)[:, OFF + 512 * c:OFF + 512 * (c + 1)],
                             start=(g == 0), stop=(g == DG - 1))
                    mm = t.matmul(bank(2 + c % 2, 512, 0, 16),
                                  wxp_sb[:, 32 * g + 16:32 * g + 32],
                                  XN(g)[:, OFF + 512 * c:OFF + 512 * (c + 1)],
                                  start=(g == 0), stop=(g == DG - 1))
                mm.then_inc(TS.h, 1)
            TS.c += 1
            bc_mm.append(TS.c)

            @A
            def _(a):
                a.wait_ge(TS.h, bc_mm[c])
                if c == 0:
                    a.wait_ge(CQ.h, ld_const)
                a.activation(BT[:, 512 * c:512 * (c + 1)],
                             bank(0 + c % 2, 512, 0, 16),
                             AF.Identity, bias=bxp[:, 0:1])
                a.activation(CT[:, 512 * c:512 * (c + 1)],
                             bank(2 + c % 2, 512, 0, 16),
                             AF.Identity, bias=bxp[:, 1:2]).then_inc(AS.h, 1)
            AS.c += 1
            bc_ev.append(AS.c)
        bc_done = AS.c

        @T
        def _(t, snap=x0_done, ksnap=VS.c):
            t.wait_ge(VS.h, snap)
            t.wait_ge(VS.h, ksnap)
            for c in range(4):
                mm = t.matmul(bank(4 + c, 512), ones1[:], X0[:, 512 * c:512 * (c + 1)],
                              start=True, stop=True)
            mm.then_inc(TS.h, 1)
        TS.c += 1
        x0rep_done = TS.c

        for g in range(DG):
            @V
            def _(v):
                if g == 0:
                    v.wait_ge(TS.h, x0rep_done)
                for c in range(4):
                    v.tensor_scalar(ZBF[:, 512 * c:512 * (c + 1)], bank(4 + c, 512),
                                    consts[:, CWDT + g:CWDT + g + 1],
                                    consts[:, CBDT + g:CBDT + g + 1],
                                    AOT.mult, AOT.add)
                v.engine_nop().then_inc(VS.h, 1)
            VS.c += 1
            zs = VS.c

            @A
            def _(a):
                a.wait_ge(VS.h, zs)
                a.activation(A1[:, 0:2048], ZBF[:, 0:2048], AF.Abs)
                a.drain()
                a.activation(A1[:, 0:2048], A1[:, 0:2048], AF.Exp,
                             scale=consts[:, CNEG1:CNEG1 + 1])
                a.drain()
                a.activation(A1[:, 0:2048], A1[:, 0:2048], AF.Ln, bias=1.0)
                a.drain()
                a.activation(ZBF[:, 0:2048], ZBF[:, 0:2048], AF.Relu).then_inc(AS.h, 1)
            AS.c += 1
            sp = AS.c

            @V
            def _(v):
                v.wait_ge(AS.h, sp)
                if g == 0:
                    v.wait_ge(TS.h, bc_mm[3])
                v.tensor_tensor(DT(g)[:], ZBF[:, 0:2048], A1[:, 0:2048], AOT.add)
                v.drain()
                v.tensor_tensor(XN(g)[:, OFF:], DT(g)[:], XN(g)[:, OFF:], AOT.mult
                                ).then_inc(VS.h, 1)
            VS.c += 1
        dtx_done = VS.c

        if debug:
            @S
            def _(s):
                s.wait_ge(VS.h, dtx_done)
                s.wait_ge(AS.h, bc_done)
                for g in range(DG):
                    s.dma_start(out=ap["dbg_dt"][128 * g:128 * (g + 1), :],
                                in_=DT(g)[:]).then_inc(LQ.h, 16)
                    LQ.c += 16
                s.dma_start(out=ap["dbg_bc"][0:16, :], in_=BT[:]).then_inc(LQ.h, 16)
                s.dma_start(out=ap["dbg_bc"][16:32, :], in_=CT[:]).then_inc(LQ.h, 16)
                LQ.c += 32

        if phases <= 3:
            _fin(S, G, LQ, GQ, AQ, GS, CQ, DSL, SSL, FIN, HS, GD, SCT)
            return nc

        def DAW(i): return pool[:, 33280 + 1024 * i:33280 + 1024 * (i + 1)]
        def PRW(i): return pool[:, 35328 + 1024 * i:35328 + 1024 * (i + 1)]
        UUW = pool[:, 37376:38400]
        SSW = pool[:, 38400:39424]
        YGW = pool[:, 46848:48896]
        u_done = []
        y_store = []
        pr_used = [GP[0].c, GP[1].c]
        for g in range(DG):
            for tc2 in range(2):
                for n in range(16):
                    kun = (g * 2 + tc2) * 16 + n
                    bb = 4 * (n % 2)

                    @T
                    def _(t):
                        if kun == 0:
                            t.wait_ge(AS.h, bc_done)
                            t.wait_ge(CQ.h, ld_const)
                        if kun <= 1:
                            t.wait_ge(VS.h, dtx_done)
                        if kun >= 2:
                            t.wait_ge(VS.h, u_done[kun - 2])
                        for c in range(2):
                            t.matmul(bank(bb + c, 512),
                                     eyerep[:, 128 * n:128 * (n + 1)],
                                     BT[:, 1024 * tc2 + 512 * c:
                                        1024 * tc2 + 512 * (c + 1)],
                                     start=True, stop=True)
                            mm = t.matmul(bank(bb + 2 + c, 512),
                                          eyerep[:, 128 * n:128 * (n + 1)],
                                          CT[:, 1024 * tc2 + 512 * c:
                                             1024 * tc2 + 512 * (c + 1)],
                                          start=True, stop=True)
                        mm.then_inc(TS.h, 1)
                    TS.c += 1
                    bsnap = TS.c

                    @A
                    def _(a):
                        if kun == 0:
                            a.wait_ge(VS.h, dtx_done)
                        if kun >= 2:
                            a.wait_ge(VS.h, u_done[kun - 2])
                        a.activation(DAW(kun % 2),
                                     DT(g)[:, 1024 * tc2:1024 * (tc2 + 1)],
                                     AF.Exp,
                                     scale=acoef[:, NST * g + n:NST * g + n + 1]
                                     ).then_inc(AS.h, 1)
                    AS.c += 1
                    da_snap = AS.c

                    @V
                    def _(v):
                        v.wait_ge(TS.h, bsnap)
                        v.wait_ge(AS.h, da_snap)
                        if kun >= 2:
                            v.wait_ge(GP[kun % 2].h, pr_used[kun % 2])
                        if n == 0 and tc2 == 0 and g >= 1:
                            v.wait_ge(LQ.h, y_store[g - 1])
                        base = OFF + 1024 * tc2
                        for c in range(2):
                            v.tensor_tensor(
                                UUW[:, 512 * c:512 * (c + 1)],
                                XN(g)[:, base + 512 * c:base + 512 * (c + 1)],
                                bank(bb + c, 512), AOT.mult)
                        v.drain()
                        init = (0.0 if tc2 == 0
                                else SCOL[:, NST * g + n:NST * g + n + 1])
                        v.tensor_tensor_scan(SSW, DAW(kun % 2), UUW, init,
                                             AOT.mult, AOT.add)
                        v.drain()
                        v.tensor_copy(SCOL[:, NST * g + n:NST * g + n + 1],
                                      SSW[:, 1023:1024])
                        tgt = (YGW[:, 1024 * tc2:1024 * (tc2 + 1)]
                               if n == 0 else PRW(kun % 2))
                        for c in range(2):
                            mm = v.tensor_tensor(
                                tgt[:, 512 * c:512 * (c + 1)],
                                SSW[:, 512 * c:512 * (c + 1)],
                                bank(bb + 2 + c, 512), AOT.mult)
                        mm.then_inc(VS.h, 2)
                    VS.c += 2
                    u_done.append(VS.c)

                    if n > 0:
                        @G
                        def _(g_):
                            g_.wait_ge(VS.h, u_done[kun])
                            g_.drain()
                            g_.tensor_tensor(YGW[:, 1024 * tc2:1024 * (tc2 + 1)],
                                             YGW[:, 1024 * tc2:1024 * (tc2 + 1)],
                                             PRW(kun % 2), AOT.add
                                             ).then_inc(GP[kun % 2].h, 1)
                        GP[kun % 2].c += 1
                        pr_used[kun % 2] = GP[kun % 2].c
            gpsnap = (GP[0].c, GP[1].c)
            vsnap = VS.c

            @S
            def _(s, gg=g, gpsnap=gpsnap, vsnap=vsnap):
                s.wait_ge(GP[0].h, gpsnap[0])
                s.wait_ge(GP[1].h, gpsnap[1])
                s.wait_ge(VS.h, vsnap)
                s.dma_start(out=ap["ybuf"][128 * gg:128 * (gg + 1), :], in_=YGW[:]
                            ).then_inc(LQ.h, 16)
                LQ.c += 16
            y_store.append(LQ.c)
        scan_done_v = VS.c
        scan_y_ld = LQ.c

        @G
        def _(g_):
            g_.wait_ge(VS.h, scan_done_v)
            g_.dma_start(out=ap["stag_in"], in_=SCOL[:]).then_inc(GQ.h, 16)
            GQ.c += 16
            g_.wait_ge(GQ.h, GQ.c)
            g_.collective_compute(
                "AllGather", AOT.bypass, replica_groups=[list(range(NCORES))],
                ins=[ap["stag_in"].opt()],
                outs=[ap["stag_out"].opt()],
            ).then_inc(CC.h, 1)
            CC.c += 1
            g_.wait_ge(CC.h, CC.c)
            g_.memset(SPART[:], 0.0).then_inc(GS.h, 1)
        GS.c += 1
        ag_done = GS.c

        spart_base = VS.c
        shb_slot = [DSL[0].c, DSL[1].c]

        @S
        def _(s, snap=CC.c):
            s.wait_ge(CC.h, snap)
            for r in range(NCORES):
                if r >= 2:
                    s.wait_ge(VS.h, spart_base + (r - 1))
                s.dma_start(out=SHB(r % 2)[:, 0:128],
                            in_=ap["stag_out"][128 * r:128 * (r + 1), :]
                            ).then_inc(DSL[r % 2].h, 16)
                DSL[r % 2].c += 16

        @V
        def _(v):
            v.wait_ge(GS.h, ag_done)
            v.wait_ge(CQ.h, ld_const)
            for r in range(NCORES):
                shb_slot[r % 2] += 16
                v.wait_ge(DSL[r % 2].h, shb_slot[r % 2])
                v.scalar_tensor_tensor(SPART[:], SHB(r % 2)[:, 0:128],
                                       consts[:, CSELP + r:CSELP + r + 1],
                                       SPART[:], AOT.mult, AOT.add).then_inc(VS.h, 1)
                v.drain()
                VS.c += 1
        spart_done = VS.c

        def CUM(g): return pool[:, 2112 * g:2112 * g + 2048]
        Y8 = DT

        cum_vs = []
        for g in range(DG):
            @V
            def _(v):
                if g == 0:
                    v.wait_ge(VS.h, spart_done)
                    v.wait_ge(LQ.h, scan_y_ld)
                v.drain()
                v.tensor_tensor_scan(CUM(g)[:], DT(g)[:], DT(g)[:], 0.0,
                                     AOT.add, AOT.bypass).then_inc(VS.h, 1)
            VS.c += 1
            cum_vs.append(VS.c)
        cum_done = VS.c

        @S
        def _(s):
            s.wait_ge(VS.h, cum_done)
            s.wait_ge(LQ.h, scan_y_ld)
            for g in range(DG):
                s.dma_start(out=Y8(g)[:], in_=ap["ybuf"][128 * g:128 * (g + 1), :]
                            ).then_inc(LQ.h, 16)
                LQ.c += 16
        y8_ld = LQ.c

        corr_done = []
        crep_mm = []
        for n in range(16):
            @T
            def _(t):
                if n == 0:
                    t.wait_ge(VS.h, cum_done)
                if n >= 2:
                    t.wait_ge(VS.h, corr_done[n * DG - 2 * DG + DG - 1])
                bb = 4 * (n % 2)
                for c in range(4):
                    mm = t.matmul(bank(bb + c, 512),
                                  eyerep[:, 128 * n:128 * (n + 1)],
                                  CT[:, 512 * c:512 * (c + 1)],
                                  start=True, stop=True)
                mm.then_inc(TS.h, 1)
            TS.c += 1
            crep_mm.append(TS.c)

            for g in range(DG):
                kgn = n * DG + g

                A1b = pool[:, 44800 + 2048 * (kgn % 2):44800 + 2048 * (kgn % 2) + 2048]

                @A
                def _(a):
                    if kgn == 0:
                        a.wait_ge(VS.h, cum_done)
                    if kgn == 1:
                        a.wait_ge(LQ.h, scan_y_ld)
                    if kgn >= 2:
                        a.wait_ge(VS.h, corr_done[kgn - 2])
                    a.activation(A1b, CUM(g)[:], AF.Exp,
                                 scale=acoef[:, NST * g + n:NST * g + n + 1]
                                 ).then_inc(AS.h, 1)
                AS.c += 1
                e_done = AS.c

                @V
                def _(v):
                    v.wait_ge(AS.h, e_done)
                    v.wait_ge(TS.h, crep_mm[n])
                    if kgn == 0:
                        v.wait_ge(LQ.h, y8_ld)
                    bb = 4 * (n % 2)
                    for c in range(4):
                        v.tensor_tensor(A1b[:, 512 * c:512 * (c + 1)],
                                        A1b[:, 512 * c:512 * (c + 1)],
                                        bank(bb + c, 512), AOT.mult)
                    v.drain()
                    v.scalar_tensor_tensor(Y8(g)[:], A1b,
                                           SPART[:, NST * g + n:NST * g + n + 1],
                                           Y8(g)[:], AOT.mult, AOT.add).then_inc(VS.h, 1)
                VS.c += 1
                corr_done.append(VS.c)
        corr_vs = VS.c
        corr_lq = LQ.c

        if debug:
            @S
            def _(s):
                s.wait_ge(VS.h, corr_vs)
                for g in range(DG):
                    s.dma_start(out=ap["dbg_y"][128 * g:128 * (g + 1), :],
                                in_=Y8(g)[:]).then_inc(LQ.h, 16)
                    LQ.c += 16

        if phases <= 6:
            _fin(S, G, LQ, GQ, AQ, GS, CQ, DSL, SSL, FIN, HS, GD, SCT)
            return nc

        @S
        def _(s):
            s.wait_ge(VS.h, corr_vs)
            for g in range(DG):
                s.dma_start(out=WOUT(g).rearrange("p (k c) -> p k c", k=8),
                            in_=ap["wouts"][g]).then_inc(LQ.h, 16)
                LQ.c += 16
        ldo_done = LQ.c

        op_mm = []
        opev_as = AS.c
        for tc in range(4):
            @T
            def _(t):
                t.wait_ge(LQ.h, ldo_done)
                if tc == 0:
                    t.wait_ge(VS.h, corr_vs)
                if tc >= 1:
                    t.wait_ge(AS.h, opev_as)
                for o in range(DG):
                    for g in range(DG):
                        mm = t.matmul(bank(o, 512), WOUT(o)[:, 128 * g:128 * (g + 1)],
                                      Y8(g)[:, 512 * tc:512 * (tc + 1)],
                                      start=(g == 0), stop=(g == DG - 1))
                mm.then_inc(TS.h, 1)
            TS.c += 1
            op_mm.append(TS.c)
            mm_snap = TS.c

            for o in range(DG):
                @A
                def _(a):
                    a.wait_ge(TS.h, mm_snap)
                    if tc == 0 and o == 0:
                        a.wait_ge(VS.h, spart_done)
                        a.wait_ge(CQ.h, ld_const)
                    if tc > 0 or o >= 2:
                        a.wait_ge(SSL[o % 2].h, SSL[o % 2].c)
                    a.activation(EV(o % 2), bank(o, 512), AF.Identity,
                                 bias=consts[:, CBOUT + o:CBOUT + o + 1]
                                 ).then_inc(AS.h, 1)
                AS.c += 1
                ev_snap = AS.c

                @G
                def _(g_):
                    g_.wait_ge(AS.h, ev_snap)
                    g_.dma_start(out=ap["xres"][128 * o:128 * (o + 1),
                                                OFF + 512 * tc:OFF + 512 * (tc + 1)],
                                 in_=EV(o % 2), accum_op=AOT.add
                                 ).then_inc(SSL[o % 2].h, 16)
                SSL[o % 2].c += 16
            opev_as = AS.c
        if debug:
            @S
            def _(s):
                s.wait_ge(SSL[0].h, SSL[0].c)
                s.wait_ge(SSL[1].h, SSL[1].c)
                s.dma_start(out=ap["dbg_x1"], in_=ap["xres"]).then_inc(LQ.h, 16)
                LQ.c += 16

        if phases <= 7:
            _fin(S, G, LQ, GQ, AQ, GS, CQ, DSL, SSL, FIN, HS, GD, SCT)
            return nc

        ln1_pre = [(SSL[0], SSL[0].c), (SSL[1], SSL[1].c), (GQ, GQ.c)]
        ln1_tpre = [(AS, AS.c), (SSL[0], SSL[0].c), (SSL[1], SSL[1].c),
                    (TS, TS.c)]
        emit_layernorm(ap["xres"][:, OFF:], TOK, XN1,
                       pre_wait=ln1_pre, t_pre=ln1_tpre)
        xn1_done = VS.c

        if debug:
            @S
            def _(s, snap=VS.c):
                s.wait_ge(VS.h, snap)
                for g in range(DG):
                    s.dma_start(out=ap["dbg_xn1"][128 * g:128 * (g + 1), :],
                                in_=XN1(g)[:]).then_inc(LQ.h, 16)
                    LQ.c += 16

        if phases <= 8:
            _fin(S, G, LQ, GQ, AQ, GS, CQ, DSL, SSL, FIN, HS, GD, SCT)
            return nc

        LGT = pool[0:8, 40704:42752]
        MT = pool[0:8, 43328:45376]
        POS = pool[0:8, 45376:47424]
        ILS = pool[:, 47424:47448].bitcast(U32)
        TKID = pool[:, 47448:47472].bitcast(U32)[:, 0:16]
        RW = pool[:, 47472:47536]
        LTOK = pool[:, 47536:47664]
        MK = pool[:, 47664:47792]
        PTOK = pool[:, 47792:47920]
        VMX = pool[:, 47920:47936]
        GEB = pool[:, 47936:47952]
        PCK = pool[:, 47952:47968]
        NDN = pool[:, 47968:47984]
        EID = pool[:, 47984:48000]
        SLT = pool[:, 48000:48016]
        PFL = pool[:, 48016:48032]
        PIN = pool[:, 48032:48048].bitcast(I32)
        T1I = pool[:, 48048:48064].bitcast(I32)
        T2I = pool[:, 48064:48080].bitcast(I32)
        FLT = pool[:, 48080:48096].bitcast(U32)

        def L3(e):
            return LTOK.rearrange("p (t e) -> p t e", e=8)[:, :, e:e + 1]

        def MK3(e):
            return MK.rearrange("p (t e) -> p t e", e=8)[:, :, e:e + 1]

        def PT3(e):
            return PTOK.rearrange("p (t e) -> p t e", e=8)[:, :, e:e + 1]

        @S
        def _(s, vsnap=xn1_done):
            s.wait_ge(VS.h, vsnap)
            s.dma_start(out=RW.rearrange("p (g c) -> p g c", g=8),
                        in_=ap["routw"].rearrange("(g p) c -> p g c", p=128)
                        ).then_inc(LQ.h, 16)
            LQ.c += 16
        rw_ld = LQ.c

        rt_mm = []
        rt_ev = []
        for c in range(4):
            @T
            def _(t):
                if c == 0:
                    t.wait_ge(LQ.h, rw_ld)
                    t.wait_ge(VS.h, xn1_done)
                if c >= 2:
                    t.wait_ge(AS.h, rt_ev[c - 2])
                for g in range(DG):
                    mm = t.matmul(bank(0 + c % 2, 512, 0, 8),
                                  RW[:, 8 * g:8 * (g + 1)],
                                  XN1(g)[:, 512 * c:512 * (c + 1)],
                                  start=(g == 0), stop=(g == DG - 1))
                mm.then_inc(TS.h, 1)
            TS.c += 1
            rt_mm.append(TS.c)

            @A
            def _(a):
                a.wait_ge(TS.h, rt_mm[c])
                a.activation(LGT[:, 512 * c:512 * (c + 1)], bank(0 + c % 2, 512, 0, 8),
                             AF.Identity, bias=routb[:]).then_inc(AS.h, 1)
            AS.c += 1
            rt_ev.append(AS.c)
        lgt_done = AS.c

        tr_mm = []
        for t_ in range(16):
            @T
            def _(t):
                t.wait_ge(AS.h, lgt_done)
                if t_ >= 2:
                    t.wait_ge(VS.h, ltok_cp[t_ - 2])
                t.transpose(bank(2 + t_ % 2, 8), LGT[:, 128 * t_:128 * (t_ + 1)],
                            ident[0:8, 0:8]).then_inc(TS.h, 1)
            TS.c += 1
            tr_mm.append(TS.c)
            if t_ == 0:
                ltok_cp = []

            @V
            def _(v):
                v.wait_ge(TS.h, tr_mm[t_])
                v.tensor_copy(LTOK[:, 8 * t_:8 * (t_ + 1)], bank(2 + t_ % 2, 8)
                              ).then_inc(VS.h, 1)
            VS.c += 1
            ltok_cp.append(VS.c)

        @V
        def _(v, snap=VS.c):
            v.wait_ge(VS.h, snap)
            v.tensor_copy(VMX, L3(0))
            v.memset(NDN, 1.0)
            v.memset(EID, 0.0)
            for e in range(1, 8):
                v.drain()
                v.tensor_tensor(VMX, VMX, L3(e), AOT.max)
            for e in range(8):
                v.drain()
                v.tensor_tensor(GEB, L3(e), VMX, AOT.is_ge)
                v.drain()
                v.tensor_tensor(PCK, GEB, NDN, AOT.mult)
                v.drain()
                v.tensor_copy(MK3(e), PCK)
                v.scalar_tensor_tensor(EID, PCK, float(e), EID, AOT.mult, AOT.add)
                v.drain()
                v.tensor_tensor(NDN, NDN, PCK, AOT.subtract)
            v.drain().then_inc(VS.h, 1)
        VS.c += 1
        amax_done = VS.c

        mkt_mm = []
        for t_ in range(16):
            @T
            def _(t):
                t.wait_ge(VS.h, amax_done)
                if t_ >= 2:
                    t.wait_ge(VS.h, mkt_cp[t_ - 2])
                t.transpose(bank(2 + t_ % 2, 128, 0, 8),
                            MK[:, 8 * t_:8 * (t_ + 1)], ident[:]).then_inc(TS.h, 1)
            TS.c += 1
            mkt_mm.append(TS.c)
            if t_ == 0:
                mkt_cp = []

            @V
            def _(v):
                v.wait_ge(TS.h, mkt_mm[t_])
                v.tensor_copy(MT[:, 128 * t_:128 * (t_ + 1)], bank(2 + t_ % 2, 128, 0, 8)
                              ).then_inc(VS.h, 1)
            VS.c += 1
            mkt_cp.append(VS.c)

        @V
        def _(v, snap=VS.c):
            v.wait_ge(VS.h, snap)
            v.tensor_tensor_scan(POS[:], MT[:], MT[:], 0.0, AOT.add, AOT.bypass
                                 ).then_inc(VS.h, 1)
        VS.c += 1
        pos_done = VS.c

        pt_mm = []
        for t_ in range(16):
            @T
            def _(t):
                t.wait_ge(VS.h, pos_done)
                if t_ >= 2:
                    t.wait_ge(VS.h, pt_cp[t_ - 2])
                t.transpose(bank(2 + t_ % 2, 8), POS[:, 128 * t_:128 * (t_ + 1)],
                            ident[0:8, 0:8]).then_inc(TS.h, 1)
            TS.c += 1
            pt_mm.append(TS.c)
            if t_ == 0:
                pt_cp = []

            @V
            def _(v):
                v.wait_ge(TS.h, pt_mm[t_])
                v.tensor_copy(PTOK[:, 8 * t_:8 * (t_ + 1)], bank(2 + t_ % 2, 8)
                              ).then_inc(VS.h, 1)
            VS.c += 1
            pt_cp.append(VS.c)

        @V
        def _(v, snap=VS.c):
            v.wait_ge(VS.h, snap)
            v.tensor_tensor(SLT, MK3(0), PT3(0), AOT.mult)
            for e in range(1, 8):
                v.drain()
                v.tensor_tensor(GEB, MK3(e), PT3(e), AOT.mult)
                v.drain()
                v.tensor_tensor(SLT, SLT, GEB, AOT.add)
            v.drain()
            v.tensor_scalar_sub(SLT, SLT, 1.0)
            v.drain()
            v.tensor_scalar_min(SLT, SLT, float(CAP - 1))
            v.drain()
            v.scalar_tensor_tensor(PFL, EID, float(CAP), SLT, AOT.mult, AOT.add)
            v.drain()
            v.tensor_copy(PIN, PFL)
            v.drain()
            v.tensor_single_scalar(T1I, PIN, 127, AOT.bitwise_and)
            v.drain()
            v.tensor_single_scalar(T2I, PIN, 7, AOT.arith_shift_right)
            v.drain()
            v.tensor_single_scalar(T1I, T1I, NCOLS, AOT.mult)
            v.drain()
            v.tensor_tensor(FLT.bitcast(I32), T1I, T2I, AOT.add)
            v.drain().then_inc(VS.h, 1)
        VS.c += 1
        flt_done = VS.c

        @G
        def _(g_):
            g_.iota(TKID, [[128, 16]], base=0, channel_multiplier=1)
            g_.iota(ILS, [[0, NCOLS]], base=2048, channel_multiplier=1
                    ).then_inc(GS.h, 1)
        GS.c += 1
        iota_done = GS.c

        @G
        def _(g_):
            g_.wait_ge(GS.h, iota_done)
            g_.dma_start(out=ap["ilist"].rearrange("(p c) one -> p (c one)", p=128),
                        in_=ILS).then_inc(GQ.h, 16)
            GQ.c += 16
            g_.wait_ge(GQ.h, GQ.c)
            g_.wait_ge(VS.h, flt_done)
            for t_ in range(16):
                g_.wait_ge(SCT.h, SCT.c)
                g_.indirect_dma_start(
                    out=ap["ilist"],
                    out_offset=IndirectOffsetOnAxis(FLT[:, t_:t_ + 1], 0),
                    in_=TKID[:, t_:t_ + 1], in_offset=None).then_inc(SCT.h, 16)
                SCT.c += 16
            g_.wait_ge(SCT.h, SCT.c)
            g_.dma_start(out=ILS,
                         in_=ap["ilist"].rearrange("(p c) one -> p (c one)", p=128)
                         ).then_inc(GQ.h, 16)
            GQ.c += 16
            g_.wait_ge(GQ.h, GQ.c)
            g_.nop().then_inc(GS.h, 1)
        GS.c += 1
        ilist_done = GS.c
        ILSB = ILS

        if debug:
            @S
            def _(s):
                s.wait_ge(GS.h, ilist_done)
                s.wait_ge(VS.h, flt_done)
                s.dma_start(out=ap["dbg_eid"], in_=EID).then_inc(LQ.h, 16)
                s.dma_start(out=ap["dbg_ilist"], in_=ILSB).then_inc(LQ.h, 16)
                LQ.c += 32

        if phases <= 9:
            _fin(S, G, LQ, GQ, AQ, GS, CQ, DSL, SSL, FIN, HS, GD, SCT)
            return nc

        def TTK(i): return pool[:, 16896 + 1024 * i:16896 + 1024 * (i + 1)]
        ttk_st = [HS[0].c, HS[1].c]
        stg_cp = []
        for t_ in range(16):
            tr_base = TS.c
            vbase_t = VS.c

            @T
            def _(t):
                if t_ == 0:
                    t.wait_ge(VS.h, xn1_done)
                for g in range(DG):
                    k = t_ * DG + g
                    if k >= 2:
                        if g >= 2:
                            t.wait_ge(VS.h, vbase_t + g - 1)
                        else:
                            t.wait_ge(VS.h, stg_cp[k - 2])
                    t.transpose(bank(4 + k % 2, 128), XN1(g)[:, 128 * t_:128 * (t_ + 1)],
                                ident[:]).then_inc(TS.h, 1)
                    TS.c += 1

            @V
            def _(v):
                for g in range(DG):
                    k = t_ * DG + g
                    v.wait_ge(TS.h, tr_base + g + 1)
                    if g == 0:
                        v.wait_ge(HS[t_ % 2].h, ttk_st[t_ % 2])
                    v.tensor_copy(TTK(t_ % 2)[:, 128 * g:128 * (g + 1)],
                                  bank(4 + k % 2, 128)).then_inc(VS.h, 1)
                    VS.c += 1
                    stg_cp.append(VS.c)

            @S
            def _(s, snap=VS.c):
                s.wait_ge(VS.h, snap)
                s.dma_start(out=ap["xn1tok"][128 * t_:128 * (t_ + 1), :],
                            in_=TTK(t_ % 2)[:]).then_inc(HS[t_ % 2].h, 16)
                HS[t_ % 2].c += 16
                ttk_st[t_ % 2] = HS[t_ % 2].c
        stage_vs = VS.c
        stage_ssl = (HS[0].c, HS[1].c)

        @V
        def _(v, snap=VS.c):
            v.wait_ge(VS.h, snap)
            v.wait_ge(HS[0].h, stage_ssl[0])
            v.wait_ge(HS[1].h, stage_ssl[1])
            v.memset(TTK(0)[:], 0.0)
            v.drain().then_inc(VS.h, 1)
        VS.c += 1
        zb_done = VS.c

        @S
        def _(s):
            s.wait_ge(VS.h, zb_done)
            s.dma_start(out=ap["xn1tok"][TOK:TOK + 128, :], in_=TTK(0)[:]
                        ).then_inc(LQ.h, 16)
            LQ.c += 16
            for t_ in range(17):
                s.dma_start(out=ap["moeout"][128 * t_:128 * (t_ + 1), :],
                            in_=TTK(0)[:]).then_inc(LQ.h, 16)
                LQ.c += 16
        zero_lq = LQ.c

        def VW(g): return pool[:, 33280 + 1024 * g:33280 + 1024 * (g + 1)]
        def KEV(i): return pool[:, 41472 + 2048 * i:41472 + 2048 * (i + 1)]
        def VTK(i): return pool[:, 42496 + 1024 * i:42496 + 1024 * (i + 1)]
        BVS = pool[:, 44544:45568]
        QW = [pool[:, 45568:46592], pool[:, 45568:46592]]
        BVR = pool[0:1, 41472:42496]

        qk_mm = []
        qk_ev = []
        qw_slot = [DSL[2].c, DSL[3].c]
        kev_st = [HS[0].c, HS[1].c]
        for o in range(16):
            @S
            def _(s):
                if o == 0:
                    s.wait_ge(VS.h, zb_done)
                if o >= 1:
                    s.wait_ge(TS.h, qk_mm[o - 1])
                s.dma_start(out=QW[0].rearrange("p (g c) -> p g c", g=8),
                            in_=ap["wqks"][o]).then_inc(DSL[2].h, 16)
                DSL[2].c += 16

            for c in range(4):
                @T
                def _(t):
                    if c == 0:
                        qw_slot[0] += 16
                        t.wait_ge(DSL[2].h, qw_slot[0])
                        t.wait_ge(VS.h, xn1_done)
                    if o * 4 + c >= 2:
                        t.wait_ge(AS.h, qk_ev[(o * 4 + c) - 2])
                    for g in range(DG):
                        mm = t.matmul(bank((o * 4 + c) % 2, 512),
                                      QW[0][:, 128 * g:128 * (g + 1)],
                                      XN1(g)[:, 512 * c:512 * (c + 1)],
                                      start=(g == 0), stop=(g == DG - 1))
                    mm.then_inc(TS.h, 1)
                TS.c += 1
                qk_mm.append(TS.c) if c == 3 else None
                if c == 3:
                    pass
                mm_snap = TS.c

                @A
                def _(a):
                    a.wait_ge(TS.h, mm_snap)
                    if o < 8:
                        dst = QT(o)[:, 512 * c:512 * (c + 1)]
                    else:
                        dst = KEV((o - 8) % 2)[:, 512 * c:512 * (c + 1)]
                    if o >= 8 and c == 0:
                        a.wait_ge(HS[(o - 8) % 2].h, kev_st[(o - 8) % 2])
                    if o < 8 and c == 0 and o < 2:
                        a.wait_ge(HS[0].h, stage_ssl[0])
                        a.wait_ge(HS[1].h, stage_ssl[1])
                        a.wait_ge(LQ.h, zero_lq)
                    a.activation(dst, bank((o * 4 + c) % 2, 512), AF.Identity,
                                 bias=consts[:, CBQKV + o:CBQKV + o + 1]
                                 ).then_inc(AS.h, 1)
                AS.c += 1
                qk_ev.append(AS.c)
            if len(qk_mm) <= o:
                qk_mm.append(TS.c)

            if o >= 8:
                @S
                def _(s, snap=AS.c):
                    s.wait_ge(AS.h, snap)
                    for cb in range(2):
                        s.dma_start(
                            out=ap["kag_in"][1024 * cb + 128 * (o - 8):
                                             1024 * cb + 128 * (o - 8) + 128, :],
                            in_=KEV((o - 8) % 2)[:, 1024 * cb:1024 * (cb + 1)]
                            ).then_inc(HS[(o - 8) % 2].h, 16)
                        HS[(o - 8) % 2].c += 16
                    kev_st[(o - 8) % 2] = HS[(o - 8) % 2].c
        qk_done_as = AS.c
        kag_ssl = (HS[0].c, HS[1].c)

        @S
        def _(s, snap=AS.c):
            s.wait_ge(AS.h, snap)
            for g in range(DG):
                s.dma_start(out=VW(g)[:],
                            in_=ap["wqkv"][128 * g:128 * (g + 1), 2048:3072]
                            ).then_inc(LQ.h, 16)
                LQ.c += 16
        vw_ld = LQ.c

        @S
        def _(s, snap=AS.c):
            s.wait_ge(AS.h, snap)
            s.wait_ge(HS[0].h, kag_ssl[0])
            s.wait_ge(HS[1].h, kag_ssl[1])
            s.dma_start(out=BVR, in_=ap["bv_row"]).then_inc(LQ.h, 16)
            LQ.c += 16
        bv_ld = LQ.c

        @T
        def _(t, snap=AS.c):
            t.wait_ge(LQ.h, bv_ld)
            t.wait_ge(AS.h, snap)
            t.matmul(bank(6, 512), ones1[:], BVR[:, 0:512], start=True,
                     stop=True)
            mm = t.matmul(bank(7, 512), ones1[:], BVR[:, 512:1024], start=True,
                          stop=True)
            mm.then_inc(TS.h, 1)
        TS.c += 1
        bv_rep = TS.c

        @V
        def _(v):
            v.wait_ge(TS.h, bv_rep)
            v.tensor_copy(BVS[:, 0:512], bank(6, 512))
            v.tensor_copy(BVS[:, 512:1024], bank(7, 512)).then_inc(VS.h, 1)
        VS.c += 1
        bv_sb = VS.c

        vt_mm = []
        vt_ev = []
        vt_st = [HS[0].c, HS[1].c]
        for t_ in range(16):
            @T
            def _(t):
                if t_ == 0:
                    t.wait_ge(VS.h, bv_sb)
                    t.wait_ge(LQ.h, bv_ld)
                if t_ >= 2:
                    t.wait_ge(VS.h, vt_ev[t_ - 2])
                for oc in range(2):
                    for g in range(DG):
                        mm = t.matmul(bank(oc + t_ % 2 * 2, 512),
                                      XN1(g)[:, 128 * t_:128 * (t_ + 1)],
                                      VW(g)[:, 512 * oc:512 * (oc + 1)],
                                      start=(g == 0), stop=(g == DG - 1))
                mm.then_inc(TS.h, 1)
            TS.c += 1
            vt_mm.append(TS.c)

            @V
            def _(v):
                v.wait_ge(TS.h, vt_mm[t_])
                if t_ >= 2:
                    v.wait_ge(HS[t_ % 2].h, vt_st[t_ % 2])
                for oc in range(2):
                    v.tensor_tensor(VTK(t_ % 2)[:, 512 * oc:512 * (oc + 1)],
                                    bank(oc + t_ % 2 * 2, 512),
                                    BVS[:, 512 * oc:512 * (oc + 1)], AOT.add
                                    ).then_inc(VS.h, 1)
                    VS.c += 1
            vt_ev.append(VS.c)

            @S
            def _(s, snap=VS.c):
                s.wait_ge(VS.h, snap)
                s.dma_start(out=ap["vag_in"][128 * t_:128 * (t_ + 1), :],
                            in_=VTK(t_ % 2)[:]).then_inc(HS[t_ % 2].h, 16)
                HS[t_ % 2].c += 16
                vt_st[t_ % 2] = HS[t_ % 2].c
        v_done_ssl = (HS[0].c, HS[1].c)

        @G
        def _(g_):
            g_.wait_ge(HS[0].h, v_done_ssl[0])
            g_.wait_ge(HS[1].h, v_done_ssl[1])
            g_.collective_compute(
                "AllGather", AOT.bypass, replica_groups=[list(range(NCORES))],
                ins=[ap["kag_in"].opt()], outs=[ap["kag_out"].opt()],
            ).then_inc(CC.h, 1)
            CC.c += 1
            g_.collective_compute(
                "AllGather", AOT.bypass, replica_groups=[list(range(NCORES))],
                ins=[ap["vag_in"].opt()], outs=[ap["vag_out"].opt()],
            ).then_inc(CC.h, 1)
            CC.c += 1
        kvag_done = CC.c

        if debug:
            @S
            def _(s):
                s.wait_ge(AS.h, qk_done_as)
                for g in range(DG):
                    s.dma_start(out=ap["dbg_q"][128 * g:128 * (g + 1), :],
                                in_=QT(g)[:]).then_inc(LQ.h, 16)
                    LQ.c += 16

        if phases <= 11:
            _fin(S, G, LQ, GQ, AQ, GS, CQ, DSL, SSL, FIN, HS, GD, SCT)
            return nc

        def XGT(g): return pool[:, 384 * g:384 * (g + 1)]
        def XGTOK(i): return pool[:, 3072 + 1024 * i:3072 + 1024 * (i + 1)]
        def WAb(i): return pool[:, 5120 + 2048 * i:5120 + 2048 * i + 1024]
        def WGb(i): return pool[:, 5120 + 2048 * i + 1024:5120 + 2048 * (i + 1)]
        def W2S(i): return pool[:, 9216 + 1024 * i:9216 + 1024 * (i + 1)]
        def SIGB(i): return pool[:, 11264 + 384 * i:11264 + 384 * (i + 1)]
        B1S = pool[:, 12032:12096]
        B2S = pool[:, 12096:12104]
        def YET(o): return pool[:, 12160 + 384 * o:12160 + 384 * (o + 1)]
        def YTOK(i): return pool[:, 3072 + 1024 * i:3072 + 1024 * (i + 1)]
        GATED = pool[:, 33280:45568]

        xg_slot = [GD[0].c, GD[1].c]
        w_slot = [DSL[2].c, DSL[3].c]
        yt_sc = [SCT.c, SCT.c]
        moe_tr = []
        moe_gate = []
        prev_w2mm = None
        for e in range(E):
            vb_e = VS.c

            @G
            def _(g_):
                g_.wait_ge(GS.h, ilist_done)
                if e == 0:
                    g_.wait_ge(VS.h, xn1_done)
                    g_.wait_ge(LQ.h, zero_lq)
                    g_.wait_ge(CC.h, kvag_done)
                else:
                    g_.wait_ge(SCT.h, SCT.c)
                for k in range(CAPT):
                    ek = e * CAPT + k
                    if k >= 2:
                        g_.wait_ge(VS.h, vb_e + 8 * (k - 1))
                    elif ek >= 2:
                        g_.wait_ge(VS.h, moe_tr[ek - 2])
                    g_.indirect_dma_start(
                        out=XGTOK(ek % 2)[:], out_offset=None,
                        in_=ap["xn1tok"],
                        in_offset=IndirectOffsetOnAxis(ILSB[:, ek:ek + 1], 0)
                        ).then_inc(GD[ek % 2].h, 16)
                    GD[ek % 2].c += 16

            for k in range(CAPT):
                ek = e * CAPT + k
                trb = TS.c

                vbk = VS.c

                @T
                def _(t):
                    xg_slot[ek % 2] += 16
                    t.wait_ge(GD[ek % 2].h, xg_slot[ek % 2])
                    if ek >= 2:
                        t.wait_ge(VS.h, moe_tr[ek - 2])
                    if e > 0 and k == 0:
                        t.wait_ge(AS.h, prev_yet_ev)
                    for g in range(DG):
                        if g >= 2:
                            t.wait_ge(VS.h, vbk + g - 1)
                        elif ek >= 1:
                            t.wait_ge(VS.h, moe_tr[ek - 1] - (1 - g))
                        t.transpose(bank(6 + g % 2, 128),
                                    XGTOK(ek % 2)[:, 128 * g:128 * (g + 1)],
                                    ident[:]).then_inc(TS.h, 1)
                        TS.c += 1

                @V
                def _(v):
                    for g in range(DG):
                        v.wait_ge(TS.h, trb + g + 1)
                        v.tensor_copy(XGT(g)[:, 128 * k:128 * (k + 1)],
                                      bank(6 + g % 2, 128)).then_inc(VS.h, 1)
                        VS.c += 1
                moe_tr.append(VS.c)
            xg_done = VS.c

            @S
            def _(s, esnap=xg_done):
                s.wait_ge(VS.h, esnap)
                s.dma_start(out=B1S, in_=ap["b1_r"][e]).then_inc(LQ.h, 16)
                s.dma_start(out=B2S, in_=ap["b2_r"][e]).then_inc(LQ.h, 16)
                LQ.c += 32
            b12_ld = LQ.c

            for pas in range(2):
                olo = 4 * pas
                for m in range(32):
                    em = (e * 2 + pas) * 32 + m

                    @S
                    def _(s):
                        if em >= 2:
                            s.wait_ge(TS.h, moe_w2mm[em - 2])
                        if pas == 0:
                            s.dma_start(
                                out=pool[:, 5120 + 2048 * (m % 2):
                                         5120 + 2048 * (m % 2) + 2048].rearrange(
                                    "p (g c) -> p g c", g=16),
                                in_=ap["w1c"][e, m]).then_inc(DSL[2 + m % 2].h, 16)
                            DSL[2 + m % 2].c += 16
                        s.dma_start(out=W2S(m % 2),
                                    in_=ap["w2"][e, 128 * m:128 * (m + 1), :]
                                    ).then_inc(DSL[2 + m % 2].h, 16)
                        DSL[2 + m % 2].c += 16

                    if em == 0:
                        moe_w2mm = []
                        moe_gate = []
                    w_slot[m % 2] = DSL[2 + m % 2].c
                    ge = e * 32 + m

                    if pas == 0:
                        @T
                        def _(t):
                            t.wait_ge(DSL[2 + m % 2].h, w_slot[m % 2])
                            if m == 0:
                                t.wait_ge(LQ.h, b12_ld)
                                t.wait_ge(VS.h, xg_done)
                            if ge >= 2:
                                t.wait_ge(VS.h, moe_gate[ge - 2])
                            for g in range(DG):
                                t.matmul(bank(0, 384), WAb(m % 2)[:, 128 * g:128 * (g + 1)],
                                         XGT(g)[:], start=(g == 0), stop=(g == DG - 1))
                            for g in range(DG):
                                mm = t.matmul(bank(1, 384),
                                              WGb(m % 2)[:, 128 * g:128 * (g + 1)],
                                              XGT(g)[:], start=(g == 0), stop=(g == DG - 1))
                            mm.then_inc(TS.h, 1)
                        TS.c += 1
                        w1_snap = TS.c

                        @A
                        def _(a):
                            a.wait_ge(TS.h, w1_snap)
                            if ge >= 2:
                                a.wait_ge(VS.h, moe_gate[ge - 2])
                            a.activation(SIGB(m % 2), bank(1, 384), AF.Sigmoid,
                                         bias=B1S[:, 32 + m:33 + m]).then_inc(AS.h, 1)
                        AS.c += 1
                        sig_snap = AS.c

                        @V
                        def _(v):
                            v.wait_ge(AS.h, sig_snap)
                            v.scalar_tensor_tensor(GATED[:, 384 * m:384 * (m + 1)],
                                                   bank(0, 384), B1S[:, m:m + 1],
                                                   SIGB(m % 2), AOT.add, AOT.mult
                                                   ).then_inc(VS.h, 1)
                        VS.c += 1
                        moe_gate.append(VS.c)

                    @T
                    def _(t):
                        if pas == 1:
                            t.wait_ge(DSL[2 + m % 2].h, w_slot[m % 2])
                        if pas == 0:
                            t.wait_ge(VS.h, moe_gate[ge])
                        elif m == 0:
                            t.wait_ge(VS.h, moe_gate[-1])
                        if m == 0 and pas == 1:
                            t.wait_ge(AS.h, yet_ev)
                        for o in range(4):
                            mm = t.matmul(bank(2 + o, 384),
                                          W2S(m % 2)[:, 128 * (olo + o):128 * (olo + o + 1)],
                                          GATED[:, 384 * m:384 * (m + 1)],
                                          start=(m == 0), stop=(m == 31))
                        mm.then_inc(TS.h, 1)
                    TS.c += 1
                    moe_w2mm.append(TS.c)
                w2_snap = TS.c

                @A
                def _(a):
                    a.wait_ge(TS.h, w2_snap)
                    for o in range(4):
                        a.activation(YET(olo + o), bank(2 + o, 384), AF.Identity,
                                     bias=B2S[:, olo + o:olo + o + 1])
                    a.drain().then_inc(AS.h, 1)
                AS.c += 1
                yet_ev = AS.c
            prev_yet_ev = AS.c

            for k in range(CAPT):
                ek2 = e * CAPT + k
                trb2 = TS.c

                if e == 0 and k == 0:
                    moe_ytr = []
                vbk2 = VS.c

                @T
                def _(t):
                    t.wait_ge(AS.h, prev_yet_ev)
                    if k == 0 and e > 0:
                        t.wait_ge(VS.h, moe_ytr[-1])
                    for o in range(DG):
                        if o >= 2:
                            t.wait_ge(VS.h, vbk2 + o - 1)
                        elif k >= 1:
                            t.wait_ge(VS.h, moe_ytr[-1] - (1 - o))
                        t.transpose(bank(6 + o % 2, 128),
                                    YET(o)[:, 128 * k:128 * (k + 1)],
                                    ident[:]).then_inc(TS.h, 1)
                        TS.c += 1

                @V
                def _(v):
                    for o in range(DG):
                        v.wait_ge(TS.h, trb2 + o + 1)
                        if o == 0:
                            v.wait_ge(SCT.h, yt_sc[ek2 % 2])
                        v.tensor_copy(YTOK(ek2 % 2)[:, 128 * o:128 * (o + 1)],
                                      bank(6 + o % 2, 128)).then_inc(VS.h, 1)
                        VS.c += 1
                moe_ytr.append(VS.c)

                @G
                def _(g_, snap=VS.c):
                    g_.wait_ge(VS.h, snap)
                    g_.wait_ge(SCT.h, SCT.c)
                    g_.indirect_dma_start(
                        out=ap["moeout"],
                        out_offset=IndirectOffsetOnAxis(ILSB[:, ek2:ek2 + 1], 0),
                        in_=YTOK(ek2 % 2)[:], in_offset=None
                        ).then_inc(SCT.h, 16)
                SCT.c += 16
                yt_sc[ek2 % 2] = SCT.c
        moe_ssl = (SCT, SCT.c)
        moe_vs = VS.c

        if debug:
            @S
            def _(s):
                s.wait_ge(SCT.h, moe_ssl[1])
                s.dma_start(out=ap["dbg_moe"], in_=ap["moeout"]).then_inc(LQ.h, 16)
                LQ.c += 16

        if phases <= 12:
            _fin(S, G, LQ, GQ, AQ, GS, CQ, DSL, SSL, FIN, HS, GD, SCT)
            return nc

        def VQb(i): return pool[:, 8192 * i:8192 * (i + 1)]
        VIDX = pool[:, 16384:16416].bitcast(U32)
        KIDX = pool[:, 16416:16448].bitcast(U32)
        def KH(i): return pool[:, 33280 + 1024 * i:33280 + 1024 * (i + 1)]
        L0row = pool[0:1, 35328:37376]
        def PHA(i): return pool[:, 37376 + 512 * i:37376 + 512 * (i + 1)]
        def PHBb(i): return pool[:, 38400 + 512 * i:38400 + 512 * (i + 1)]
        VA16 = pool[:, 39424:40464]
        def OACC(i): return pool[0:65, 43520 + 2048 * i:43520 + 2048 * (i + 1)]
        ONPB = pool[:, 41472:43520]
        ONP = QT
        CQSC = 74

        @S
        def _(s, snap=moe_vs):
            s.wait_ge(VS.h, snap)
            s.dma_start(out=VIDX, in_=ap["vqidx"]).then_inc(LQ.h, 16)
            s.dma_start(out=KIDX, in_=ap["kidx"]).then_inc(LQ.h, 16)
            LQ.c += 32
        idx_ld = LQ.c

        @G
        def _(g_):
            g_.wait_ge(VS.h, moe_vs)
            g_.wait_ge(CC.h, kvag_done)
            g_.wait_ge(SCT.h, moe_ssl[1])
            g_.memset(consts[:, CQSC:CQSC + 1], 0.125)
            for j in range(16):
                g_.memset(VA16[:, 65 * j + 64:65 * j + 65], 1.0)
            g_.nop().then_inc(GS.h, 1)
        GS.c += 1
        att_init = GS.c

        vq_slot = [GD[0].c, GD[1].c]
        kh_slot = [GD[2].c, GD[3].c]
        onp_done = []
        for hp in range(8):
            for q in range(4):
                @G
                def _(g_):
                    if hp == 0 and q == 0:
                        g_.wait_ge(GS.h, att_init)
                        g_.wait_ge(LQ.h, idx_ld)
                    if hp > 0 or q >= 2:
                        g_.wait_ge(TS.h, vaq_guard[(hp * 4 + q) - 2])
                    for j in range(DG):
                        g_.indirect_dma_start(
                            out=VQb(q % 2)[:, 1024 * j:1024 * (j + 1)],
                            out_offset=None, in_=ap["vag_out"],
                            in_offset=IndirectOffsetOnAxis(
                                VIDX[:, 8 * q + j:8 * q + j + 1], 0)
                            ).then_inc(GD[q % 2].h, 16)
                        GD[q % 2].c += 16
                    g_.indirect_dma_start(
                        out=KH(q % 2)[:], out_offset=None,
                        in_=ap["kag_out"],
                        in_offset=IndirectOffsetOnAxis(
                            KIDX[:, 4 * hp + q:4 * hp + q + 1], 0)
                        ).then_inc(GD[2 + q % 2].h, 16)
                    GD[2 + q % 2].c += 16
                if hp == 0 and q == 0:
                    vaq_guard = []
                vq_slot[q % 2] = GD[q % 2].c
                kh_slot[q % 2] = GD[2 + q % 2].c

                @V
                def _(v):
                    v.wait_ge(GD[q % 2].h, vq_slot[q % 2])
                    if hp + q > 0:
                        v.wait_ge(TS.h, pv_last)
                    for i in range(2):
                        hh = 2 * hp + i
                        for j in range(DG):
                            v.tensor_copy(VA16[:, 65 * (8 * i + j):65 * (8 * i + j) + 64],
                                          VQb(q % 2)[:, 1024 * j + 64 * hh:
                                                     1024 * j + 64 * hh + 64])
                    v.drain().then_inc(VS.h, 1)
                VS.c += 1
                va_cp = VS.c

                for qc in range(4):
                    exp_as = []
                    st_mm = []
                    pv_mm = []

                    def emit_S(j, qc=qc):
                        @T
                        def _(t):
                            if j == 0 and qc == 0:
                                t.wait_ge(GD[2 + q % 2].h, kh_slot[q % 2])
                                t.wait_ge(VS.h, va_cp)
                                if q == 0 and hp > 0:
                                    t.wait_ge(VS.h, onp_done[-1])
                            if j >= 2:
                                t.wait_ge(AS.h, exp_as[2 * (j - 2)])
                                t.wait_ge(AS.h, exp_as[2 * (j - 2) + 1])
                            t.matmul(bank(0 + 4 * (j % 2), 512),
                                     KH(q % 2)[0:64, 128 * j:128 * (j + 1)],
                                     QT(hp)[0:64, 512 * qc:512 * (qc + 1)],
                                     start=True, stop=True,
                                     tile_position=(0, 0)).then_inc(TS.h, 1)
                            TS.c += 1
                            t.matmul(bank(1 + 4 * (j % 2), 512),
                                     KH(q % 2)[64:128, 128 * j:128 * (j + 1)],
                                     QT(hp)[64:128, 512 * qc:512 * (qc + 1)],
                                     start=True, stop=True,
                                     tile_position=(64, 0)).then_inc(TS.h, 1)
                            TS.c += 1
                            st_mm.append(TS.c)

                    def emit_E(j):
                        @A
                        def _(a):
                            a.wait_ge(TS.h, st_mm[j])
                            if j >= 2:
                                a.wait_ge(TS.h, pv_mm[2 * (j - 2) + 1])
                            a.activation(PHA(j % 2), bank(0 + 4 * (j % 2), 512),
                                         AF.Exp, scale=consts[:, CQSC:CQSC + 1]
                                         ).then_inc(AS.h, 1)
                            AS.c += 1
                            exp_as.append(AS.c)
                            a.activation(PHBb(j % 2), bank(1 + 4 * (j % 2), 512),
                                         AF.Exp, scale=consts[:, CQSC:CQSC + 1]
                                         ).then_inc(AS.h, 1)
                            AS.c += 1
                            exp_as.append(AS.c)

                    def emit_PV(j, qc=qc):
                        @T
                        def _(t):
                            t.wait_ge(AS.h, exp_as[2 * j + 1])
                            if j == 0 and (hp + q + qc > 0):
                                t.wait_ge(VS.h, oev_last)
                            t.matmul(bank(2, 512, 0, 65),
                                     VA16[:, 65 * j:65 * j + 65],
                                     PHA(j % 2),
                                     start=(j == 0), stop=(j == DG - 1)
                                     ).then_inc(TS.h, 1)
                            TS.c += 1
                            t.matmul(bank(3, 512, 0, 65),
                                     VA16[:, 65 * (8 + j):65 * (8 + j) + 65],
                                     PHBb(j % 2),
                                     start=(j == 0), stop=(j == DG - 1)
                                     ).then_inc(TS.h, 1)
                            TS.c += 1
                            pv_mm.append(TS.c - 1)
                            pv_mm.append(TS.c)

                    for j in range(DG):
                        emit_S(j)
                        emit_E(j)
                        if j >= 1:
                            emit_PV(j - 1)
                    emit_PV(DG - 1)
                    pv_last = TS.c

                    @V
                    def _(v, qc=qc):
                        v.wait_ge(TS.h, pv_last)
                        v.drain()
                        for i in range(2):
                            if q == 0:
                                v.tensor_copy(OACC(i)[:, 512 * qc:512 * (qc + 1)],
                                              bank(2 + i, 512, 0, 65))
                            else:
                                v.tensor_tensor(OACC(i)[:, 512 * qc:512 * (qc + 1)],
                                                OACC(i)[:, 512 * qc:512 * (qc + 1)],
                                                bank(2 + i, 512, 0, 65), AOT.add)
                        v.drain().then_inc(VS.h, 1)
                    VS.c += 1
                    oev_last = VS.c
                vaq_guard.append(TS.c)

            for i in range(2):
                @V
                def _(v, snap=VS.c):
                    v.wait_ge(VS.h, snap)
                    v.drain()
                    v.reciprocal(OACC(i)[64:65, :], OACC(i)[64:65, :]
                                 ).then_inc(VS.h, 1)
                VS.c += 1
                rec_done = VS.c

                @G
                def _(g_):
                    g_.wait_ge(VS.h, rec_done)
                    g_.dma_start(out=L0row, in_=OACC(i)[64:65, :]).then_inc(GQ.h, 16)
                    GQ.c += 16
                l0_ld = GQ.c

                norm_vs = []
                for qc in range(4):
                    @T
                    def _(t):
                        t.wait_ge(GQ.h, l0_ld)
                        if qc >= 1:
                            t.wait_ge(VS.h, norm_vs[qc - 1])
                        t.matmul(bank(6, 512, 0, 64), ones1[0:1, 0:64],
                                 L0row[:, 512 * qc:512 * (qc + 1)],
                                 start=True, stop=True).then_inc(TS.h, 1)
                    TS.c += 1
                    lr_mm = TS.c

                    @V
                    def _(v):
                        v.wait_ge(TS.h, lr_mm)
                        dst = (ONP(hp)[0:64, 512 * qc:512 * (qc + 1)] if i == 0
                               else ONPB[0:64, 512 * qc:512 * (qc + 1)])
                        v.tensor_tensor(dst, OACC(i)[0:64, 512 * qc:512 * (qc + 1)],
                                        bank(6, 512, 0, 64), AOT.mult
                                        ).then_inc(VS.h, 1)
                    VS.c += 1
                    norm_vs.append(VS.c)

                if i == 1:
                    @G
                    def _(g_, snap=VS.c):
                        g_.wait_ge(VS.h, snap)
                        g_.dma_start(out=ONP(hp)[64:128, :], in_=ONPB[0:64, :]
                                     ).then_inc(GQ.h, 16)
                        GQ.c += 16
                    onp_gq = GQ.c

                    @V
                    def _(v):
                        v.wait_ge(GQ.h, onp_gq)
                        v.engine_nop().then_inc(VS.h, 1)
                    VS.c += 1
            onp_done.append(VS.c)
        att_done_vs = VS.c

        def WOT(g): return pool[:, 1024 * g:1024 * (g + 1)]
        EVB = [pool[:, 8192 + 512 * i:8192 + 512 * (i + 1)] for i in range(2)]

        @S
        def _(s, snap=att_done_vs):
            s.wait_ge(VS.h, snap)
            for g in range(DG):
                s.dma_start(out=WOT(g)[:], in_=ap["wo"][128 * g:128 * (g + 1), :]
                            ).then_inc(LQ.h, 16)
                LQ.c += 16
        wo_ld = LQ.c

        op2_ssl = [SSL[0].c, SSL[1].c]
        for o in range(DG):
            for qc in range(4):
                oq = o * 4 + qc

                @T
                def _(t):
                    if oq == 0:
                        t.wait_ge(LQ.h, wo_ld)
                        t.wait_ge(VS.h, att_done_vs)
                    if oq >= 2:
                        t.wait_ge(AS.h, op2_ev[oq - 2])
                    for gp in range(DG):
                        mm = t.matmul(bank(oq % 2, 512), WOT(gp)[:, 128 * o:128 * (o + 1)],
                                      ONP(gp)[:, 512 * qc:512 * (qc + 1)],
                                      start=(gp == 0), stop=(gp == DG - 1))
                    mm.then_inc(TS.h, 1)
                TS.c += 1
                op2_mm = TS.c
                if oq == 0:
                    op2_ev = []

                @A
                def _(a):
                    a.wait_ge(TS.h, op2_mm)
                    if oq >= 2:
                        a.wait_ge(SSL[oq % 2].h, op2_ssl[oq % 2])
                    a.activation(EVB[oq % 2], bank(oq % 2, 512), AF.Identity,
                                 bias=consts[:, CBO + o:CBO + o + 1]).then_inc(AS.h, 1)
                AS.c += 1
                op2_ev.append(AS.c)

                @G
                def _(g_, snap=AS.c):
                    g_.wait_ge(AS.h, snap)
                    g_.dma_start(out=ap["xres"][128 * o:128 * (o + 1),
                                                OFF + 512 * qc:OFF + 512 * (qc + 1)],
                                 in_=EVB[oq % 2], accum_op=AOT.add
                                 ).then_inc(SSL[oq % 2].h, 16)
                SSL[oq % 2].c += 16
                op2_ssl[oq % 2] = SSL[oq % 2].c
        att_add_ssl = (SSL[0].c, SSL[1].c)

        if debug:
            @S
            def _(s):
                s.wait_ge(SSL[0].h, att_add_ssl[0])
                s.wait_ge(SSL[1].h, att_add_ssl[1])
                s.dma_start(out=ap["dbg_att"], in_=ap["xres"]).then_inc(LQ.h, 16)
                LQ.c += 16

        def XR2(g): return pool[:, 16896 + 2048 * g:16896 + 2048 * (g + 1)]
        def MOT(i): return pool[:, 1024 * i:1024 * (i + 1)]
        def OB(i): return pool[:, 2048 + 1024 * i:2048 + 1024 * (i + 1)]

        @S
        def _(s):
            s.wait_ge(SSL[0].h, att_add_ssl[0])
            s.wait_ge(SSL[1].h, att_add_ssl[1])
            s.wait_ge(TS.h, TS.c)
            for g in range(DG):
                s.dma_start(out=XR2(g)[:], in_=ap["xres"][128 * g:128 * (g + 1), OFF:]
                            ).then_inc(LQ.h, 16)
                LQ.c += 16
        xr2_ld = LQ.c

        mot_slot = [DSL[0].c, DSL[1].c]
        ob_st = [HS[0].c, HS[1].c]
        fin_cp = []
        for t_ in range(16):
            @S
            def _(s):
                if t_ == 0:
                    s.wait_ge(SCT.h, moe_ssl[1])
                if t_ >= 2:
                    s.wait_ge(VS.h, fin_cp[t_ - 2])
                s.dma_start(out=MOT(t_ % 2)[:],
                            in_=ap["moeout"][128 * t_:128 * (t_ + 1), :]
                            ).then_inc(DSL[t_ % 2].h, 16)
                DSL[t_ % 2].c += 16

            trf = TS.c

            vbf = VS.c

            @T
            def _(t):
                if t_ == 0:
                    t.wait_ge(LQ.h, xr2_ld)
                if t_ >= 2:
                    t.wait_ge(VS.h, fin_cp[t_ - 2])
                for g in range(DG):
                    if g >= 2:
                        t.wait_ge(VS.h, vbf + g - 1)
                    elif t_ >= 1:
                        t.wait_ge(VS.h, fin_cp[-1] - (1 - g))
                    t.transpose(bank(g % 2, 128), XR2(g)[:, 128 * t_:128 * (t_ + 1)],
                                ident[:]).then_inc(TS.h, 1)
                    TS.c += 1

            @V
            def _(v):
                mot_slot[t_ % 2] += 16
                v.wait_ge(DSL[t_ % 2].h, mot_slot[t_ % 2])
                if t_ >= 2:
                    v.wait_ge(HS[t_ % 2].h, ob_st[t_ % 2])
                for g in range(DG):
                    v.wait_ge(TS.h, trf + g + 1)
                    v.tensor_tensor(OB(t_ % 2)[:, 128 * g:128 * (g + 1)],
                                    bank(g % 2, 128),
                                    MOT(t_ % 2)[:, 128 * g:128 * (g + 1)], AOT.add
                                    ).then_inc(VS.h, 1)
                    VS.c += 1
            fin_cp.append(VS.c)

            @S
            def _(s, snap=VS.c):
                s.wait_ge(VS.h, snap)
                s.dma_start(out=ap["out"][128 * t_:128 * (t_ + 1), :],
                            in_=OB(t_ % 2)[:]).then_inc(HS[t_ % 2].h, 16)
                HS[t_ % 2].c += 16
                ob_st[t_ % 2] = HS[t_ % 2].c

        _fin(S, G, LQ, GQ, AQ, GS, CQ, DSL, SSL, FIN, HS, GD, SCT)
    return nc


def _fin(S, G, LQ, GQ, AQ, GS, CQ=None, DSL=(), SSL=(), FIN=None, HS=(), GD=(),
         SCT=None):
    @G
    def _(g):
        g.wait_ge(GQ.h, GQ.c)
        for x in SSL:
            g.wait_ge(x.h, x.c)
        for x in GD:
            g.wait_ge(x.h, x.c)
        if SCT is not None:
            g.wait_ge(SCT.h, SCT.c)
        g.nop().then_inc(FIN.h, 1)
    FIN.c += 1

    @S
    def _(s):
        s.wait_ge(LQ.h, LQ.c)
        s.wait_ge(AQ.h, AQ.c)
        if CQ is not None:
            s.wait_ge(CQ.h, CQ.c)
        for x in DSL:
            s.wait_ge(x.h, x.c)
        for x in HS:
            s.wait_ge(x.h, x.c)
        s.wait_ge(FIN.h, FIN.c)



def make_inmaps(inp):
    x = np.ascontiguousarray(np.asarray(inp["L0_attn_x"], np.float32))
    ident = np.eye(128, dtype=np.float32)
    eyerep = np.zeros((16, 16 * 128), np.float32)
    for n in range(16):
        eyerep[n, 128 * n:128 * (n + 1)] = 1.0
    alog = np.asarray(inp["L0_attn_A_log"], np.float32)
    alog_r = np.zeros((128, DG * NST), np.float32)
    for g in range(DG):
        alog_r[:, NST * g:NST * (g + 1)] = alog[128 * g:128 * (g + 1), :]

    def colsplit(v):
        return np.ascontiguousarray(np.asarray(v, np.float32).reshape(DG, 128).T)

    bqkv = np.asarray(inp["L1_attn_b_qkv"], np.float32)

    def slice_lhsT(wT):
        k_in, n_out = wT.shape
        return np.ascontiguousarray(
            wT.reshape(k_in // 128, 128, n_out // 128, 128).transpose(2, 1, 0, 3))

    wrT = np.asarray(inp["L0_ffn_W_r"], np.float32).T
    wkT = np.asarray(inp["L0_ffn_W_k"], np.float32).T
    wrk = slice_lhsT(np.ascontiguousarray(wrT))
    wkk = slice_lhsT(np.ascontiguousarray(wkT))
    wrks = np.ascontiguousarray(np.concatenate([wrk, wkk], axis=2))
    wvs = slice_lhsT(np.ascontiguousarray(np.asarray(inp["L0_ffn_W_v"], np.float32).T))
    wouts = slice_lhsT(np.ascontiguousarray(
        np.asarray(inp["L0_attn_W_out"], np.float32).T))
    wqks = slice_lhsT(np.ascontiguousarray(
        np.asarray(inp["L1_attn_W_qkv"], np.float32)[:, 0:2048]))
    w1 = np.asarray(inp["L1_ffn_w1"], np.float32)
    w1a = w1[:, :, 0:4096].reshape(E, 8, 128, 32, 128).transpose(0, 3, 2, 1, 4)
    w1g = w1[:, :, 4096:8192].reshape(E, 8, 128, 32, 128).transpose(0, 3, 2, 1, 4)
    w1c = np.ascontiguousarray(np.concatenate([w1a, w1g], axis=3))
    com = {
        "ident": ident, "eyerep": eyerep, "alog_r": alog_r,
        "wdt_r": colsplit(inp["L0_attn_W_dt"][:, 0]),
        "bdt_r": colsplit(inp["L0_attn_b_dt"]),
        "mur_r": colsplit(inp["L0_ffn_mu_r"]),
        "muk_r": colsplit(inp["L0_ffn_mu_k"]),
        "bout_r": colsplit(inp["L0_attn_b_out"]),
        "bo_r": colsplit(inp["L1_attn_b_o"]),
        "bqkv_r": np.ascontiguousarray(bqkv[:2048].reshape(16, 128).T),
        "bv_row": np.ascontiguousarray(bqkv[2048:].reshape(1, 1024)),
        "wxpT": np.ascontiguousarray(np.asarray(inp["L0_attn_W_xp"], np.float32).T),
        "wrks": wrks, "wvs": wvs, "wouts": wouts, "wqks": wqks, "w1c": w1c,
        "bxp": np.ascontiguousarray(
            np.asarray(inp["L0_attn_b_xp"], np.float32).reshape(2, 16).T),
        "wqkv": np.ascontiguousarray(np.asarray(inp["L1_attn_W_qkv"], np.float32)),
        "wo": np.ascontiguousarray(np.asarray(inp["L1_attn_W_o"], np.float32)),
        "routw": np.ascontiguousarray(np.asarray(inp["L1_ffn_router_w"], np.float32)),
        "routb": np.asarray(inp["L1_ffn_router_b"], np.float32).reshape(E, 1),
        "b1_r": np.ascontiguousarray(
            np.asarray(inp["L1_ffn_b1"], np.float32).reshape(E, 64, 128
                                                             ).transpose(0, 2, 1)),
        "w2": np.ascontiguousarray(np.asarray(inp["L1_ffn_w2"], np.float32)),
        "b2_r": np.ascontiguousarray(
            np.asarray(inp["L1_ffn_b2"], np.float32).reshape(E, DG, 128
                                                             ).transpose(0, 2, 1)),
    }

    maps = []
    prng = np.arange(128, dtype=np.uint32)
    for c in range(NCORES):
        b, h = c // 2, c % 2
        start = h * TOK
        xT = np.zeros((D, NTOT), np.float32)
        xT[:, OFF:] = x[b, start:start + TOK].T
        if start > 0:
            xT[:, OFF - 1] = x[b, start - 1]
        selp = np.zeros((128, NCORES), np.float32)
        if h == 1:
            selp[:, c - 1] = 1.0
        kidx = np.zeros((128, 32), np.uint32)
        for hp in range(8):
            for q in range(4):
                base = (2 * b + q // 2) * 2048 + (q % 2) * 1024
                kidx[:64, hp * 4 + q] = base + 64 * (2 * hp) + prng[:64]
                kidx[64:, hp * 4 + q] = base + 64 * (2 * hp + 1) + prng[:64]
        vqidx = np.zeros((128, 32), np.uint32)
        for q in range(4):
            for j in range(8):
                base = (2 * b + q // 2) * 2048 + (q % 2) * 1024 + 128 * j
                vqidx[:, q * 8 + j] = base + prng
        m = dict(com)
        m.update({"xT": xT, "selp": selp, "kidx": kidx, "vqidx": vqidx})
        maps.append(m)
    return maps


def kernel(**inputs):
    nc = build_nc(debug=False)
    maps = make_inmaps(inputs)
    res = run_bass_kernel_spmd(nc, maps, core_ids=list(range(NCORES)))
    out = np.zeros((4, 4096, 1024), np.float32)
    for c in range(NCORES):
        b, h = c // 2, c % 2
        out[b, h * TOK:(h + 1) * TOK, :] = res.results[c]["out"]
    return out


# version 07 (final)
# speedup vs baseline: 1.0133x; 1.0133x over previous
import numpy as np
import concourse.bass as bass
import concourse.mybir as mybir
from concourse.bass_utils import run_bass_kernel_spmd
from concourse.bass_core_rust import IndirectOffsetOnAxis

F32 = mybir.dt.float32
U32 = mybir.dt.uint32
I32 = mybir.dt.int32
AOT = mybir.AluOpType
AF = mybir.ActivationFunctionType

TOK = 2048
OFF = 64
NTOT = TOK + OFF
D = 1024
DG = 8
NST = 16
HEADS = 16
HS = 64
E = 8
CAP = 384
CAPT = CAP // 128
NCOLS = E * CAPT
HID2 = 8192
HID = 4096
NCORES = 8
EPS = 1e-5
POOL = 48896


class Cnt:
    def __init__(self, h):
        self.h = h
        self.c = 0


def build_nc(debug=False, phases=99):
    nc = bass.Bass("TRN2", target_bir_lowering=False)
    ap = {}
    th = {}

    def din(name, shape, dtype=F32):
        th[name] = nc.dram_tensor(name, shape, dtype, kind="ExternalInput")
        ap[name] = th[name].ap()

    def dout(name, shape, dtype=F32):
        th[name] = nc.dram_tensor(name, shape, dtype, kind="ExternalOutput")
        ap[name] = th[name].ap()

    def dint(name, shape, dtype=F32):
        th[name] = nc.dram_tensor(name, shape, dtype)
        ap[name] = th[name].ap()

    din("xT", [D, NTOT])
    din("ident", [128, 128])
    din("eyerep", [16, 16 * 128])
    din("alog_r", [128, DG * NST])
    din("wdt_r", [128, DG]); din("bdt_r", [128, DG])
    din("mur_r", [128, DG]); din("muk_r", [128, DG])
    din("wxpT", [D, 2 * NST]); din("bxp", [NST, 2])
    din("wouts", [DG, 128, 8, 128]); din("bout_r", [128, DG])
    din("wrks", [DG, 128, 16, 128]); din("wvs", [DG, 128, 8, 128])
    din("wqks", [16, 128, 8, 128]); din("wqkv", [D, 3 * D])
    din("bqkv_r", [128, 16]); din("bv_row", [1, D])
    din("wo", [D, D]); din("bo_r", [128, DG])
    din("routw", [D, E]); din("routb", [E, 1])
    din("w1c", [E, 32, 128, 16, 128]); din("b1_r", [E, 128, 64])
    din("w2", [E, HID, D]); din("b2_r", [E, 128, DG])
    din("selp", [128, NCORES])
    din("kidx", [128, 32], U32)
    din("vqidx", [128, 32], U32)

    dout("out", [TOK, D])

    dint("xres", [D, NTOT])
    dint("ybuf", [D, TOK])
    dint("kag_in", [2 * D, D])
    dint("kag_out", [NCORES * 2 * D, D])
    dint("vag_in", [TOK, D])
    dint("vag_out", [NCORES * TOK, D])
    dint("stag_in", [128, 128])
    dint("stag_out", [NCORES * 128, 128])
    dint("xn1tok", [TOK + 128, D])
    dint("moeout", [TOK + 128, D])
    dint("ilist", [128 * NCOLS, 1], U32)

    if debug:
        dout("dbg_xn", [D, NTOT])
        dout("dbg_dt", [D, TOK])
        dout("dbg_bc", [32, TOK])
        dout("dbg_y", [D, TOK])
        dout("dbg_x1", [D, NTOT])
        dout("dbg_xn1", [D, TOK])
        dout("dbg_eid", [128, 16])
        dout("dbg_ilist", [128, NCOLS], U32)
        dout("dbg_q", [D, TOK])
        dout("dbg_moe", [TOK + 128, D])
        dout("dbg_att", [D, NTOT])
        dout("dbg_av", [D, TOK])
        dout("dbg_oacc", [2 * 65, TOK])

    import contextlib
    st = contextlib.ExitStack()
    with st:
        block = st.enter_context(nc.Block())
        T = block.tensor; V = block.vector; A = block.scalar
        G = block.gpsimd; S = block.sync

        LQ = Cnt(st.enter_context(nc.semaphore("LQ")))
        GQ = Cnt(st.enter_context(nc.semaphore("GQ")))
        AQ = Cnt(st.enter_context(nc.semaphore("AQ")))
        TS = Cnt(st.enter_context(nc.semaphore("TS")))
        VS = Cnt(st.enter_context(nc.semaphore("VS")))
        AS = Cnt(st.enter_context(nc.semaphore("AS")))
        GS = Cnt(st.enter_context(nc.semaphore("GS")))
        CC = Cnt(st.enter_context(nc.semaphore("CC")))
        CQ = Cnt(st.enter_context(nc.semaphore("CQ")))
        DSL = [Cnt(st.enter_context(nc.semaphore(f"D{i}"))) for i in range(4)]
        SSL = [Cnt(st.enter_context(nc.semaphore(f"S{i}"))) for i in range(2)]
        GP = [Cnt(st.enter_context(nc.semaphore(f"GP{i}"))) for i in range(2)]
        HS = [Cnt(st.enter_context(nc.semaphore(f"H{i}"))) for i in range(2)]
        GD = [Cnt(st.enter_context(nc.semaphore(f"G{i}"))) for i in range(4)]
        SCT = Cnt(st.enter_context(nc.semaphore("SCT")))
        FIN = Cnt(st.enter_context(nc.semaphore("FIN")))

        pool = st.enter_context(nc.sbuf_tensor("pool", [128, POOL], F32))
        pp = st.enter_context(nc.psum_tensor("pp", [128, 4096], F32))

        ident = st.enter_context(nc.sbuf_tensor("ident_sb", [128, 128], F32))
        eyerep = st.enter_context(nc.sbuf_tensor("eyerep_sb", [16, 16 * 128], F32))
        ones1 = st.enter_context(nc.sbuf_tensor("ones1_sb", [1, 128], F32))
        onesc = st.enter_context(nc.sbuf_tensor("onesc_sb", [128, 1], F32))
        consts = st.enter_context(nc.sbuf_tensor("consts_sb", [128, 80], F32))
        alog = st.enter_context(nc.sbuf_tensor("alog_sb", [128, DG * NST], F32))
        acoef = st.enter_context(nc.sbuf_tensor("acoef_sb", [128, DG * NST], F32))
        routb = st.enter_context(nc.sbuf_tensor("routb_sb", [8, 1], F32))
        bxp = st.enter_context(nc.sbuf_tensor("bxp_sb", [16, 2], F32))

        CWDT, CBDT, CMUR, CMUK, CBOUT, CBO, CSELP = 0, 8, 16, 24, 32, 40, 48
        CBQKV = 56
        CEPS, CNEG1 = 72, 73

        def bank(b, w=512, p0=0, p1=128, off=0):
            return pp[p0:p1, 512 * b + off:512 * b + off + w]

        def XN(g): return pool[:, 2112 * g:2112 * (g + 1)]
        def XN1(g): return pool[:, 2048 * g:2048 * (g + 1)]
        def DT(g): return pool[:, 16896 + 2048 * g:16896 + 2048 * (g + 1)]
        QT = DT
        def XR(g): return pool[:, 16896 + 512 * g:16896 + 512 * (g + 1)]
        def XK(g): return pool[:, 20992 + 512 * g:20992 + 512 * (g + 1)]
        def RS(g): return pool[:, 25088 + 512 * g:25088 + 512 * (g + 1)]
        def KA(g): return pool[:, 29184 + 512 * g:29184 + 512 * (g + 1)]
        def WRK(i, bf): return pool[:, 33280 + 1024 * (2 * bf + i):
                                    33280 + 1024 * (2 * bf + i) + 1024]
        def WV(bf): return pool[:, 37376 + 1024 * bf:37376 + 1024 * bf + 1024]
        wxp_sb = pool[:, 33280:33536]
        X0 = pool[0:1, 34304:36352]
        def YC(g): return pool[:, 33280 + 512 * g:33280 + 512 * (g + 1)]
        YG = pool[:, 33280:35328]
        def DA(i): return pool[:, 37376 + 512 * i:37376 + 512 * (i + 1)]
        UU = pool[:, 38400:38912]
        SSb = pool[:, 38912:39424]
        def PR(i): return pool[:, 39424 + 512 * i:39424 + 512 * (i + 1)]
        def SHB(i): return pool[:, 37376 + 512 * i:37376 + 512 * (i + 1)]
        def EV(i): return pool[:, 37376 + 512 * i:37376 + 512 * (i + 1)]
        SCOL = pool[:, 40448:40576]
        SPART = pool[:, 40576:40704]
        BT = pool[0:16, 40704:42752]
        CT = pool[0:16, 42752:44800]
        ZBF = pool[:, 44800:46848]
        A1 = pool[:, 46848:48896]
        def LNS(i): return pool[:, 39936 + 512 * i:39936 + 512 * (i + 1)]
        def WOUT(g): return pool[:, 39936 + 1024 * g:39936 + 1024 * (g + 1)]
        SR_MEAN = pool[0:1, 33280:35392]
        SR_RSTD = pool[0:1, 35392:37504]
        SR_TMP = pool[0:1, 37504:39616]

        @S
        def _(s):
            for nm, dst in [("ident", ident[:]), ("eyerep", eyerep[:]),
                            ("alog_r", alog[:]), ("routb", routb[:]),
                            ("bxp", bxp[:])]:
                s.dma_start(out=dst, in_=ap[nm]).then_inc(CQ.h, 16)
                CQ.c += 16
            for nm, col, w in [("wdt_r", CWDT, 8), ("bdt_r", CBDT, 8),
                               ("mur_r", CMUR, 8), ("muk_r", CMUK, 8),
                               ("bout_r", CBOUT, 8), ("bo_r", CBO, 8),
                               ("selp", CSELP, 8), ("bqkv_r", CBQKV, 16)]:
                s.dma_start(out=consts[:, col:col + w], in_=ap[nm]).then_inc(CQ.h, 16)
                CQ.c += 16
            s.dma_start(out=ap["xres"], in_=ap["xT"]).then_inc(CQ.h, 16)
            CQ.c += 16
        ld_const = CQ.c

        @G
        def _(g):
            g.memset(ones1[:], 1.0)
            g.memset(onesc[:], 1.0)
            g.memset(consts[:, CEPS:CEPS + 1], EPS)
            g.memset(consts[:, CNEG1:CNEG1 + 1], -1.0).then_inc(GS.h, 1)
        GS.c += 1

        @A
        def _(a):
            a.wait_ge(CQ.h, ld_const)
            a.activation(acoef[:], alog[:], AF.Exp).then_inc(AS.h, 1)
        AS.c += 1

        @V
        def _(v, snap=AS.c):
            v.wait_ge(AS.h, snap)
            v.tensor_scalar_mul(acoef[:], acoef[:], -1.0).then_inc(VS.h, 1)
        VS.c += 1

        def emit_layernorm(src_dram, W, dst, pre_wait=None, t_pre=None):
            nch = (W + 511) // 512

            def cols(c):
                lo = c * 512
                return lo, min(lo + 512, W)

            as_base = AS.c
            ts_base = TS.c
            ldslot = [DSL[0].c, DSL[1].c]
            evac = []
            for c in range(nch):
                lo, hi = cols(c)
                w = hi - lo

                @S
                def _(s):
                    if c == 0 and pre_wait is not None:
                        for sem_, val_ in pre_wait:
                            s.wait_ge(sem_.h, val_)
                    for g in range(DG):
                        k = c * DG + g
                        if k >= 2:
                            s.wait_ge(AS.h, as_base + (k - 1))
                            s.wait_ge(TS.h, ts_base + 2 * (k - 1))
                        s.dma_start(out=LNS(k % 2)[:, 0:w],
                                    in_=src_dram[128 * g:128 * (g + 1), lo:hi]
                                    ).then_inc(DSL[k % 2].h, 16)
                        DSL[k % 2].c += 16

                @A
                def _(a):
                    for g in range(DG):
                        k = c * DG + g
                        ldslot[k % 2] += 16
                        a.wait_ge(DSL[k % 2].h, ldslot[k % 2])
                        if k >= 2:
                            a.wait_ge(TS.h, ts_base + 2 * (k - 1))
                        a.activation(LNS(2 + k % 2)[:, 0:w], LNS(k % 2)[:, 0:w],
                                     AF.Square).then_inc(AS.h, 1)
                        AS.c += 1

                @T
                def _(t):
                    if c == 0:
                        t.wait_ge(GS.h, 1)
                        if t_pre is not None:
                            for sem_, val_ in t_pre:
                                t.wait_ge(sem_.h, val_)
                    if c >= 1:
                        t.wait_ge(VS.h, evac[c - 1])
                    for g in range(DG):
                        k = c * DG + g
                        t.wait_ge(AS.h, as_base + k + 1)
                        t.matmul(bank(0, w, 0, 1), onesc[:], LNS(k % 2)[:, 0:w],
                                 start=(g == 0), stop=(g == DG - 1))
                        t.matmul(bank(1, w, 0, 1), onesc[:], LNS(2 + k % 2)[:, 0:w],
                                 start=(g == 0), stop=(g == DG - 1)).then_inc(TS.h, 2)
                        TS.c += 2
                tsnap = TS.c

                @V
                def _(v):
                    v.wait_ge(TS.h, tsnap)
                    v.tensor_scalar_mul(SR_MEAN[:, lo:hi], bank(0, w, 0, 1), 1.0 / D)
                    v.tensor_scalar_mul(SR_TMP[:, lo:hi], bank(1, w, 0, 1), 1.0 / D
                                        ).then_inc(VS.h, 1)
                VS.c += 1
                evac.append(VS.c)

            vsnap = VS.c

            @V
            def _(v):
                v.wait_ge(VS.h, vsnap)
                v.tensor_tensor(SR_RSTD[:, 0:W], SR_MEAN[:, 0:W], SR_MEAN[:, 0:W],
                                AOT.mult)
                v.drain()
                v.tensor_tensor(SR_TMP[:, 0:W], SR_TMP[:, 0:W], SR_RSTD[:, 0:W],
                                AOT.subtract).then_inc(VS.h, 1)
            VS.c += 1
            v2 = VS.c

            @A
            def _(a):
                a.wait_ge(VS.h, v2)
                a.wait_ge(GS.h, 1)
                a.activation(SR_RSTD[:, 0:W], SR_TMP[:, 0:W], AF.Sqrt,
                             bias=consts[0:1, CEPS:CEPS + 1]).then_inc(AS.h, 1)
            AS.c += 1
            a2 = AS.c

            @V
            def _(v):
                v.wait_ge(AS.h, a2)
                v.reciprocal(SR_RSTD[:, 0:W], SR_RSTD[:, 0:W]).then_inc(VS.h, 1)
            VS.c += 1
            rows_ready = VS.c

            vn_base = VS.c
            ld2slot = [DSL[0].c, DSL[1].c]
            for c in range(nch):
                lo, hi = cols(c)
                w = hi - lo

                @T
                def _(t):
                    t.wait_ge(VS.h, rows_ready)
                    if c >= 1:
                        t.wait_ge(VS.h, vn_base + c * DG)
                    t.matmul(bank(2, w), ones1[:], SR_MEAN[:, lo:hi],
                             start=True, stop=True)
                    t.matmul(bank(3, w), ones1[:], SR_RSTD[:, lo:hi],
                             start=True, stop=True).then_inc(TS.h, 1)
                    TS.c += 1
                tsnap2 = TS.c

                @S
                def _(s):
                    for g in range(DG):
                        k = c * DG + g
                        if k < 2:
                            s.wait_ge(VS.h, rows_ready)
                        if k >= 2:
                            s.wait_ge(VS.h, vn_base + (k - 1))
                        s.dma_start(out=LNS(k % 2)[:, 0:w],
                                    in_=src_dram[128 * g:128 * (g + 1), lo:hi]
                                    ).then_inc(DSL[k % 2].h, 16)
                        DSL[k % 2].c += 16

                @V
                def _(v):
                    v.wait_ge(TS.h, tsnap2)
                    for g in range(DG):
                        k = c * DG + g
                        ld2slot[k % 2] += 16
                        v.wait_ge(DSL[k % 2].h, ld2slot[k % 2])
                        v.tensor_tensor(dst(g)[:, lo:hi], LNS(k % 2)[:, 0:w],
                                        bank(2, w), AOT.subtract)
                        v.drain()
                        v.tensor_tensor(dst(g)[:, lo:hi], dst(g)[:, lo:hi],
                                        bank(3, w), AOT.mult).then_inc(VS.h, 1)
                        VS.c += 1

        emit_layernorm(ap["xT"], NTOT, XN)
        xn_done = VS.c

        if debug:
            @S
            def _(s, snap=VS.c):
                s.wait_ge(VS.h, snap)
                for g in range(DG):
                    s.dma_start(out=ap["dbg_xn"][128 * g:128 * (g + 1), :],
                                in_=XN(g)[:]).then_inc(LQ.h, 16)
                    LQ.c += 16
        if phases <= 1:
            _fin(S, G, LQ, GQ, AQ, GS, CQ, DSL, SSL, FIN, HS, GD, SCT)
            return nc

        mix_done = []
        proj_done = []
        rk_mm = []
        rk_ev = []
        kv_ev = []
        kv_st = []
        for tc in range(4):
            lo = OFF + 512 * tc
            hi = lo + 512

            @V
            def _(v):
                v.wait_ge(VS.h, xn_done)
                if tc == 0:
                    v.wait_ge(CQ.h, ld_const)
                if tc >= 1:
                    v.wait_ge(TS.h, proj_done[tc - 1])
                for g in range(DG):
                    v.tensor_tensor(XR(g)[:], XN(g)[:, lo:hi],
                                    XN(g)[:, lo - 1:hi - 1], AOT.subtract)
                    v.drain()
                    v.scalar_tensor_tensor(XK(g)[:], XR(g)[:],
                                           consts[:, CMUK + g:CMUK + g + 1],
                                           XN(g)[:, lo - 1:hi - 1], AOT.mult, AOT.add)
                    v.drain()
                    v.scalar_tensor_tensor(XR(g)[:], XR(g)[:],
                                           consts[:, CMUR + g:CMUR + g + 1],
                                           XN(g)[:, lo - 1:hi - 1], AOT.mult, AOT.add
                                           ).then_inc(VS.h, 1)
                    VS.c += 1
            mix_done.append(VS.c)
            rk_ts_base = TS.c
            wrk_slot = [DSL[2].c, DSL[3].c]

            @S
            def _(s):
                for o in range(DG):
                    ko = tc * DG + o
                    if o >= 2:
                        s.wait_ge(TS.h, rk_ts_base + o - 1)
                    elif ko >= 2:
                        s.wait_ge(TS.h, rk_mm[ko - 2])
                    s.dma_start(
                        out=WRK(0, ko % 2).rearrange("p (g c) -> p g c", g=8),
                        in_=ap["wrks"][o][:, 0:8, :]).then_inc(DSL[2 + ko % 2].h, 16)
                    s.dma_start(
                        out=WRK(1, ko % 2).rearrange("p (g c) -> p g c", g=8),
                        in_=ap["wrks"][o][:, 8:16, :]).then_inc(DSL[2 + ko % 2].h, 16)
                    DSL[2 + ko % 2].c += 32

            if tc == 0:
                kv_mm = []

            for o in range(DG):
                ko = tc * DG + o

                @T
                def _(t):
                    wrk_slot[ko % 2] += 32
                    t.wait_ge(DSL[2 + ko % 2].h, wrk_slot[ko % 2])
                    t.wait_ge(VS.h, mix_done[tc])
                    if ko >= 2:
                        t.wait_ge(AS.h, rk_ev[ko - 2])
                    for g in range(DG):
                        t.matmul(bank(0 + 2 * (ko % 2), 512),
                                 WRK(0, ko % 2)[:, 128 * g:128 * (g + 1)],
                                 XR(g)[:], start=(g == 0), stop=(g == DG - 1))
                        mm = t.matmul(bank(1 + 2 * (ko % 2), 512),
                                      WRK(1, ko % 2)[:, 128 * g:128 * (g + 1)],
                                      XK(g)[:], start=(g == 0), stop=(g == DG - 1))
                    mm.then_inc(TS.h, 1)
                TS.c += 1
                rk_mm.append(TS.c)

                @A
                def _(a):
                    a.wait_ge(TS.h, rk_mm[ko])
                    if ko >= DG:
                        a.wait_ge(VS.h, kv_ev[ko - DG])
                    a.activation(RS(o)[:], bank(0 + 2 * (ko % 2), 512), AF.Sigmoid)
                    a.activation(KA(o)[:], bank(1 + 2 * (ko % 2), 512), AF.Relu
                                 ).then_inc(AS.h, 1)
                AS.c += 1
                rk_ev.append(AS.c)

            @V
            def _(v, snap=AS.c):
                v.wait_ge(AS.h, snap)
                for o in range(DG):
                    v.tensor_tensor(KA(o)[:], KA(o)[:], KA(o)[:], AOT.mult
                                    ).then_inc(VS.h, 1)
                    VS.c += 1
            ka_done = VS.c
            kv_ts_base = TS.c
            wv_slot = [DSL[0].c, DSL[1].c]

            @S
            def _(s):
                for o in range(DG):
                    ko = tc * DG + o
                    if o >= 2:
                        s.wait_ge(TS.h, kv_ts_base + o - 1)
                    elif ko >= 2:
                        s.wait_ge(TS.h, kv_mm[ko - 2])
                    s.dma_start(out=WV(ko % 2).rearrange("p (g c) -> p g c", g=8),
                                in_=ap["wvs"][o]).then_inc(DSL[ko % 2].h, 16)
                    DSL[ko % 2].c += 16

            for o in range(DG):
                ko = tc * DG + o

                @T
                def _(t):
                    wv_slot[ko % 2] += 16
                    t.wait_ge(DSL[ko % 2].h, wv_slot[ko % 2])
                    t.wait_ge(VS.h, ka_done)
                    if ko >= 2:
                        t.wait_ge(VS.h, kv_ev[ko - 2])
                    for g in range(DG):
                        mm = t.matmul(bank(4 + ko % 2, 512),
                                      WV(ko % 2)[:, 128 * g:128 * (g + 1)],
                                      KA(g)[:], start=(g == 0), stop=(g == DG - 1))
                    mm.then_inc(TS.h, 1)
                TS.c += 1
                kv_mm.append(TS.c)
                kvsnap = TS.c

                @V
                def _(v):
                    v.wait_ge(TS.h, kvsnap)
                    if ko >= 2:
                        v.wait_ge(SSL[ko % 2].h, SSL[ko % 2].c)
                    v.tensor_tensor(PR(ko % 2), bank(4 + ko % 2, 512), RS(o)[:],
                                    AOT.mult).then_inc(VS.h, 1)
                VS.c += 1
                kv_ev.append(VS.c)

                @G
                def _(g_):
                    g_.wait_ge(VS.h, kv_ev[ko])
                    g_.dma_start(out=ap["xres"][128 * o:128 * (o + 1), lo:hi],
                                 in_=PR(ko % 2), accum_op=AOT.add
                                 ).then_inc(SSL[ko % 2].h, 16)
                SSL[ko % 2].c += 16
                kv_st.append(None)
            proj_done.append(TS.c)
        ffn_gq = GQ.c
        p2_ssl = (SSL[0].c, SSL[1].c)

        if phases <= 2:
            _fin(S, G, LQ, GQ, AQ, GS, CQ, DSL, SSL, FIN, HS, GD, SCT)
            return nc

        @S
        def _(s, tsnap=TS.c):
            s.wait_ge(TS.h, tsnap)
            s.dma_start(out=wxp_sb.rearrange("p (g c) -> p g c", g=8),
                        in_=ap["wxpT"].rearrange("(g p) c -> p g c", p=128)
                        ).then_inc(LQ.h, 16)
            LQ.c += 16
        ldw_done = LQ.c

        @V
        def _(v, snap=VS.c):
            v.wait_ge(VS.h, snap)
            v.wait_ge(LQ.h, ldw_done)
            v.wait_ge(CQ.h, ld_const)
            v.tensor_copy(X0[:, 0:TOK], XN(0)[0:1, OFF:]).then_inc(VS.h, 1)
        VS.c += 1
        x0_done = VS.c

        bc_mm = []
        bc_ev = []
        for c in range(4):
            @T
            def _(t):
                if c == 0:
                    t.wait_ge(LQ.h, ldw_done)
                    t.wait_ge(AS.h, AS.c)
                if c >= 2:
                    t.wait_ge(AS.h, bc_ev[c - 2])
                for g in range(DG):
                    t.matmul(bank(0 + c % 2, 512, 0, 16),
                             wxp_sb[:, 32 * g:32 * g + 16],
                             XN(g)[:, OFF + 512 * c:OFF + 512 * (c + 1)],
                             start=(g == 0), stop=(g == DG - 1))
                    mm = t.matmul(bank(2 + c % 2, 512, 0, 16),
                                  wxp_sb[:, 32 * g + 16:32 * g + 32],
                                  XN(g)[:, OFF + 512 * c:OFF + 512 * (c + 1)],
                                  start=(g == 0), stop=(g == DG - 1))
                mm.then_inc(TS.h, 1)
            TS.c += 1
            bc_mm.append(TS.c)

            @A
            def _(a):
                a.wait_ge(TS.h, bc_mm[c])
                if c == 0:
                    a.wait_ge(CQ.h, ld_const)
                a.activation(BT[:, 512 * c:512 * (c + 1)],
                             bank(0 + c % 2, 512, 0, 16),
                             AF.Identity, bias=bxp[:, 0:1])
                a.activation(CT[:, 512 * c:512 * (c + 1)],
                             bank(2 + c % 2, 512, 0, 16),
                             AF.Identity, bias=bxp[:, 1:2]).then_inc(AS.h, 1)
            AS.c += 1
            bc_ev.append(AS.c)
        bc_done = AS.c

        @T
        def _(t, snap=x0_done, ksnap=VS.c):
            t.wait_ge(VS.h, snap)
            t.wait_ge(VS.h, ksnap)
            for c in range(4):
                mm = t.matmul(bank(4 + c, 512), ones1[:], X0[:, 512 * c:512 * (c + 1)],
                              start=True, stop=True)
            mm.then_inc(TS.h, 1)
        TS.c += 1
        x0rep_done = TS.c

        for g in range(DG):
            @V
            def _(v):
                if g == 0:
                    v.wait_ge(TS.h, x0rep_done)
                for c in range(4):
                    v.tensor_scalar(ZBF[:, 512 * c:512 * (c + 1)], bank(4 + c, 512),
                                    consts[:, CWDT + g:CWDT + g + 1],
                                    consts[:, CBDT + g:CBDT + g + 1],
                                    AOT.mult, AOT.add)
                v.engine_nop().then_inc(VS.h, 1)
            VS.c += 1
            zs = VS.c

            @A
            def _(a):
                a.wait_ge(VS.h, zs)
                a.activation(A1[:, 0:2048], ZBF[:, 0:2048], AF.Abs)
                a.drain()
                a.activation(A1[:, 0:2048], A1[:, 0:2048], AF.Exp,
                             scale=consts[:, CNEG1:CNEG1 + 1])
                a.drain()
                a.activation(A1[:, 0:2048], A1[:, 0:2048], AF.Ln, bias=1.0)
                a.drain()
                a.activation(ZBF[:, 0:2048], ZBF[:, 0:2048], AF.Relu).then_inc(AS.h, 1)
            AS.c += 1
            sp = AS.c

            @V
            def _(v):
                v.wait_ge(AS.h, sp)
                if g == 0:
                    v.wait_ge(TS.h, bc_mm[3])
                v.tensor_tensor(DT(g)[:], ZBF[:, 0:2048], A1[:, 0:2048], AOT.add)
                v.drain()
                v.tensor_tensor(XN(g)[:, OFF:], DT(g)[:], XN(g)[:, OFF:], AOT.mult
                                ).then_inc(VS.h, 1)
            VS.c += 1
        dtx_done = VS.c

        if debug:
            @S
            def _(s):
                s.wait_ge(VS.h, dtx_done)
                s.wait_ge(AS.h, bc_done)
                for g in range(DG):
                    s.dma_start(out=ap["dbg_dt"][128 * g:128 * (g + 1), :],
                                in_=DT(g)[:]).then_inc(LQ.h, 16)
                    LQ.c += 16
                s.dma_start(out=ap["dbg_bc"][0:16, :], in_=BT[:]).then_inc(LQ.h, 16)
                s.dma_start(out=ap["dbg_bc"][16:32, :], in_=CT[:]).then_inc(LQ.h, 16)
                LQ.c += 32

        if phases <= 3:
            _fin(S, G, LQ, GQ, AQ, GS, CQ, DSL, SSL, FIN, HS, GD, SCT)
            return nc

        def DAW(i): return pool[:, 33280 + 1024 * i:33280 + 1024 * (i + 1)]
        def PRW(i): return pool[:, 35328 + 1024 * i:35328 + 1024 * (i + 1)]
        UUW = pool[:, 37376:38400]
        SSW = pool[:, 38400:39424]
        YGW = pool[:, 46848:48896]
        u_done = []
        y_store = []
        pr_used = [GP[0].c, GP[1].c]
        for g in range(DG):
            for tc2 in range(2):
                for n in range(16):
                    kun = (g * 2 + tc2) * 16 + n
                    bb = 4 * (n % 2)

                    @T
                    def _(t):
                        if kun == 0:
                            t.wait_ge(AS.h, bc_done)
                            t.wait_ge(CQ.h, ld_const)
                        if kun <= 1:
                            t.wait_ge(VS.h, dtx_done)
                        if kun >= 2:
                            t.wait_ge(VS.h, u_done[kun - 2])
                        for c in range(2):
                            t.matmul(bank(bb + c, 512),
                                     eyerep[:, 128 * n:128 * (n + 1)],
                                     BT[:, 1024 * tc2 + 512 * c:
                                        1024 * tc2 + 512 * (c + 1)],
                                     start=True, stop=True)
                            mm = t.matmul(bank(bb + 2 + c, 512),
                                          eyerep[:, 128 * n:128 * (n + 1)],
                                          CT[:, 1024 * tc2 + 512 * c:
                                             1024 * tc2 + 512 * (c + 1)],
                                          start=True, stop=True)
                        mm.then_inc(TS.h, 1)
                    TS.c += 1
                    bsnap = TS.c

                    @A
                    def _(a):
                        if kun == 0:
                            a.wait_ge(VS.h, dtx_done)
                        if kun >= 2:
                            a.wait_ge(VS.h, u_done[kun - 2])
                        a.activation(DAW(kun % 2),
                                     DT(g)[:, 1024 * tc2:1024 * (tc2 + 1)],
                                     AF.Exp,
                                     scale=acoef[:, NST * g + n:NST * g + n + 1]
                                     ).then_inc(AS.h, 1)
                    AS.c += 1
                    da_snap = AS.c

                    @V
                    def _(v):
                        v.wait_ge(TS.h, bsnap)
                        v.wait_ge(AS.h, da_snap)
                        if kun >= 2:
                            v.wait_ge(GP[kun % 2].h, pr_used[kun % 2])
                        if n == 0 and tc2 == 0 and g >= 1:
                            v.wait_ge(LQ.h, y_store[g - 1])
                        base = OFF + 1024 * tc2
                        for c in range(2):
                            v.tensor_tensor(
                                UUW[:, 512 * c:512 * (c + 1)],
                                XN(g)[:, base + 512 * c:base + 512 * (c + 1)],
                                bank(bb + c, 512), AOT.mult)
                        v.drain()
                        init = (0.0 if tc2 == 0
                                else SCOL[:, NST * g + n:NST * g + n + 1])
                        v.tensor_tensor_scan(SSW, DAW(kun % 2), UUW, init,
                                             AOT.mult, AOT.add)
                        v.drain()
                        v.tensor_copy(SCOL[:, NST * g + n:NST * g + n + 1],
                                      SSW[:, 1023:1024])
                        tgt = (YGW[:, 1024 * tc2:1024 * (tc2 + 1)]
                               if n == 0 else PRW(kun % 2))
                        for c in range(2):
                            mm = v.tensor_tensor(
                                tgt[:, 512 * c:512 * (c + 1)],
                                SSW[:, 512 * c:512 * (c + 1)],
                                bank(bb + 2 + c, 512), AOT.mult)
                        mm.then_inc(VS.h, 2)
                    VS.c += 2
                    u_done.append(VS.c)

                    if n > 0:
                        @G
                        def _(g_):
                            g_.wait_ge(VS.h, u_done[kun])
                            g_.drain()
                            g_.tensor_tensor(YGW[:, 1024 * tc2:1024 * (tc2 + 1)],
                                             YGW[:, 1024 * tc2:1024 * (tc2 + 1)],
                                             PRW(kun % 2), AOT.add
                                             ).then_inc(GP[kun % 2].h, 1)
                        GP[kun % 2].c += 1
                        pr_used[kun % 2] = GP[kun % 2].c
            gpsnap = (GP[0].c, GP[1].c)
            vsnap = VS.c

            @S
            def _(s, gg=g, gpsnap=gpsnap, vsnap=vsnap):
                s.wait_ge(GP[0].h, gpsnap[0])
                s.wait_ge(GP[1].h, gpsnap[1])
                s.wait_ge(VS.h, vsnap)
                s.dma_start(out=ap["ybuf"][128 * gg:128 * (gg + 1), :], in_=YGW[:]
                            ).then_inc(LQ.h, 16)
                LQ.c += 16
            y_store.append(LQ.c)
        scan_done_v = VS.c
        scan_y_ld = LQ.c

        @G
        def _(g_):
            g_.wait_ge(VS.h, scan_done_v)
            g_.dma_start(out=ap["stag_in"], in_=SCOL[:]).then_inc(GQ.h, 16)
            GQ.c += 16
            g_.wait_ge(GQ.h, GQ.c)
            g_.collective_compute(
                "AllGather", AOT.bypass, replica_groups=[list(range(NCORES))],
                ins=[ap["stag_in"].opt()],
                outs=[ap["stag_out"].opt()],
            ).then_inc(CC.h, 1)
            CC.c += 1
            g_.wait_ge(CC.h, CC.c)
            g_.memset(SPART[:], 0.0).then_inc(GS.h, 1)
        GS.c += 1
        ag_done = GS.c

        spart_base = VS.c
        shb_slot = [DSL[0].c, DSL[1].c]

        @S
        def _(s, snap=CC.c):
            s.wait_ge(CC.h, snap)
            for r in range(NCORES):
                if r >= 2:
                    s.wait_ge(VS.h, spart_base + (r - 1))
                s.dma_start(out=SHB(r % 2)[:, 0:128],
                            in_=ap["stag_out"][128 * r:128 * (r + 1), :]
                            ).then_inc(DSL[r % 2].h, 16)
                DSL[r % 2].c += 16

        @V
        def _(v):
            v.wait_ge(GS.h, ag_done)
            v.wait_ge(CQ.h, ld_const)
            for r in range(NCORES):
                shb_slot[r % 2] += 16
                v.wait_ge(DSL[r % 2].h, shb_slot[r % 2])
                v.scalar_tensor_tensor(SPART[:], SHB(r % 2)[:, 0:128],
                                       consts[:, CSELP + r:CSELP + r + 1],
                                       SPART[:], AOT.mult, AOT.add).then_inc(VS.h, 1)
                v.drain()
                VS.c += 1
        spart_done = VS.c

        def CUM(g): return pool[:, 2112 * g:2112 * g + 2048]
        Y8 = DT

        cum_vs = []
        for g in range(DG):
            @V
            def _(v):
                if g == 0:
                    v.wait_ge(VS.h, spart_done)
                    v.wait_ge(LQ.h, scan_y_ld)
                v.drain()
                v.tensor_tensor_scan(CUM(g)[:], DT(g)[:], DT(g)[:], 0.0,
                                     AOT.add, AOT.bypass).then_inc(VS.h, 1)
            VS.c += 1
            cum_vs.append(VS.c)
        cum_done = VS.c

        @S
        def _(s):
            s.wait_ge(VS.h, cum_done)
            s.wait_ge(LQ.h, scan_y_ld)
            for g in range(DG):
                s.dma_start(out=Y8(g)[:], in_=ap["ybuf"][128 * g:128 * (g + 1), :]
                            ).then_inc(LQ.h, 16)
                LQ.c += 16
        y8_ld = LQ.c

        corr_done = []
        crep_mm = []
        for n in range(16):
            @T
            def _(t):
                if n == 0:
                    t.wait_ge(VS.h, cum_done)
                if n >= 2:
                    t.wait_ge(VS.h, corr_done[n * DG - 2 * DG + DG - 1])
                bb = 4 * (n % 2)
                for c in range(4):
                    mm = t.matmul(bank(bb + c, 512),
                                  eyerep[:, 128 * n:128 * (n + 1)],
                                  CT[:, 512 * c:512 * (c + 1)],
                                  start=True, stop=True)
                mm.then_inc(TS.h, 1)
            TS.c += 1
            crep_mm.append(TS.c)

            for g in range(DG):
                kgn = n * DG + g

                A1b = pool[:, 44800 + 2048 * (kgn % 2):44800 + 2048 * (kgn % 2) + 2048]

                @A
                def _(a):
                    if kgn == 0:
                        a.wait_ge(VS.h, cum_done)
                    if kgn == 1:
                        a.wait_ge(LQ.h, scan_y_ld)
                    if kgn >= 2:
                        a.wait_ge(VS.h, corr_done[kgn - 2])
                    a.activation(A1b, CUM(g)[:], AF.Exp,
                                 scale=acoef[:, NST * g + n:NST * g + n + 1]
                                 ).then_inc(AS.h, 1)
                AS.c += 1
                e_done = AS.c

                @V
                def _(v):
                    v.wait_ge(AS.h, e_done)
                    v.wait_ge(TS.h, crep_mm[n])
                    if kgn == 0:
                        v.wait_ge(LQ.h, y8_ld)
                    bb = 4 * (n % 2)
                    for c in range(4):
                        v.tensor_tensor(A1b[:, 512 * c:512 * (c + 1)],
                                        A1b[:, 512 * c:512 * (c + 1)],
                                        bank(bb + c, 512), AOT.mult)
                    v.drain()
                    v.scalar_tensor_tensor(Y8(g)[:], A1b,
                                           SPART[:, NST * g + n:NST * g + n + 1],
                                           Y8(g)[:], AOT.mult, AOT.add).then_inc(VS.h, 1)
                VS.c += 1
                corr_done.append(VS.c)
        corr_vs = VS.c
        corr_lq = LQ.c

        if debug:
            @S
            def _(s):
                s.wait_ge(VS.h, corr_vs)
                for g in range(DG):
                    s.dma_start(out=ap["dbg_y"][128 * g:128 * (g + 1), :],
                                in_=Y8(g)[:]).then_inc(LQ.h, 16)
                    LQ.c += 16

        if phases <= 6:
            _fin(S, G, LQ, GQ, AQ, GS, CQ, DSL, SSL, FIN, HS, GD, SCT)
            return nc

        @S
        def _(s):
            s.wait_ge(VS.h, corr_vs)
            s.wait_ge(SSL[0].h, p2_ssl[0])
            s.wait_ge(SSL[1].h, p2_ssl[1])
            for g in range(DG):
                s.dma_start(out=WOUT(g).rearrange("p (k c) -> p k c", k=8),
                            in_=ap["wouts"][g]).then_inc(LQ.h, 16)
                LQ.c += 16
        ldo_done = LQ.c

        op_mm = []
        opev_as = AS.c
        for tc in range(4):
            @T
            def _(t):
                t.wait_ge(LQ.h, ldo_done)
                if tc == 0:
                    t.wait_ge(VS.h, corr_vs)
                if tc >= 1:
                    t.wait_ge(AS.h, opev_as)
                for o in range(DG):
                    for g in range(DG):
                        mm = t.matmul(bank(o, 512), WOUT(o)[:, 128 * g:128 * (g + 1)],
                                      Y8(g)[:, 512 * tc:512 * (tc + 1)],
                                      start=(g == 0), stop=(g == DG - 1))
                mm.then_inc(TS.h, 1)
            TS.c += 1
            op_mm.append(TS.c)
            mm_snap = TS.c

            for o in range(DG):
                @A
                def _(a):
                    a.wait_ge(TS.h, mm_snap)
                    if tc == 0 and o == 0:
                        a.wait_ge(VS.h, spart_done)
                        a.wait_ge(CQ.h, ld_const)
                    if tc > 0 or o >= 2:
                        a.wait_ge(SSL[o % 2].h, SSL[o % 2].c)
                    a.activation(EV(o % 2), bank(o, 512), AF.Identity,
                                 bias=consts[:, CBOUT + o:CBOUT + o + 1]
                                 ).then_inc(AS.h, 1)
                AS.c += 1
                ev_snap = AS.c

                @G
                def _(g_):
                    g_.wait_ge(AS.h, ev_snap)
                    g_.dma_start(out=ap["xres"][128 * o:128 * (o + 1),
                                                OFF + 512 * tc:OFF + 512 * (tc + 1)],
                                 in_=EV(o % 2), accum_op=AOT.add
                                 ).then_inc(SSL[o % 2].h, 16)
                SSL[o % 2].c += 16
            opev_as = AS.c
        if debug:
            @S
            def _(s):
                s.wait_ge(SSL[0].h, SSL[0].c)
                s.wait_ge(SSL[1].h, SSL[1].c)
                s.dma_start(out=ap["dbg_x1"], in_=ap["xres"]).then_inc(LQ.h, 16)
                LQ.c += 16

        if phases <= 7:
            _fin(S, G, LQ, GQ, AQ, GS, CQ, DSL, SSL, FIN, HS, GD, SCT)
            return nc

        ln1_pre = [(SSL[0], SSL[0].c), (SSL[1], SSL[1].c), (GQ, GQ.c)]
        ln1_tpre = [(AS, AS.c), (SSL[0], SSL[0].c), (SSL[1], SSL[1].c),
                    (TS, TS.c)]
        emit_layernorm(ap["xres"][:, OFF:], TOK, XN1,
                       pre_wait=ln1_pre, t_pre=ln1_tpre)
        xn1_done = VS.c

        if debug:
            @S
            def _(s, snap=VS.c):
                s.wait_ge(VS.h, snap)
                for g in range(DG):
                    s.dma_start(out=ap["dbg_xn1"][128 * g:128 * (g + 1), :],
                                in_=XN1(g)[:]).then_inc(LQ.h, 16)
                    LQ.c += 16

        if phases <= 8:
            _fin(S, G, LQ, GQ, AQ, GS, CQ, DSL, SSL, FIN, HS, GD, SCT)
            return nc

        LGT = pool[0:8, 40704:42752]
        MT = pool[0:8, 43328:45376]
        POS = pool[0:8, 45376:47424]
        ILS = pool[:, 47424:47448].bitcast(U32)
        TKID = pool[:, 47448:47472].bitcast(U32)[:, 0:16]
        RW = pool[:, 47472:47536]
        LTOK = pool[:, 47536:47664]
        MK = pool[:, 47664:47792]
        PTOK = pool[:, 47792:47920]
        VMX = pool[:, 47920:47936]
        GEB = pool[:, 47936:47952]
        PCK = pool[:, 47952:47968]
        NDN = pool[:, 47968:47984]
        EID = pool[:, 47984:48000]
        SLT = pool[:, 48000:48016]
        PFL = pool[:, 48016:48032]
        PIN = pool[:, 48032:48048].bitcast(I32)
        T1I = pool[:, 48048:48064].bitcast(I32)
        T2I = pool[:, 48064:48080].bitcast(I32)
        FLT = pool[:, 48080:48096].bitcast(U32)

        def L3(e):
            return LTOK.rearrange("p (t e) -> p t e", e=8)[:, :, e:e + 1]

        def MK3(e):
            return MK.rearrange("p (t e) -> p t e", e=8)[:, :, e:e + 1]

        def PT3(e):
            return PTOK.rearrange("p (t e) -> p t e", e=8)[:, :, e:e + 1]

        @S
        def _(s, vsnap=xn1_done):
            s.wait_ge(VS.h, vsnap)
            s.dma_start(out=RW.rearrange("p (g c) -> p g c", g=8),
                        in_=ap["routw"].rearrange("(g p) c -> p g c", p=128)
                        ).then_inc(LQ.h, 16)
            LQ.c += 16
        rw_ld = LQ.c

        rt_mm = []
        rt_ev = []
        for c in range(4):
            @T
            def _(t):
                if c == 0:
                    t.wait_ge(LQ.h, rw_ld)
                    t.wait_ge(VS.h, xn1_done)
                if c >= 2:
                    t.wait_ge(AS.h, rt_ev[c - 2])
                for g in range(DG):
                    mm = t.matmul(bank(0 + c % 2, 512, 0, 8),
                                  RW[:, 8 * g:8 * (g + 1)],
                                  XN1(g)[:, 512 * c:512 * (c + 1)],
                                  start=(g == 0), stop=(g == DG - 1))
                mm.then_inc(TS.h, 1)
            TS.c += 1
            rt_mm.append(TS.c)

            @A
            def _(a):
                a.wait_ge(TS.h, rt_mm[c])
                a.activation(LGT[:, 512 * c:512 * (c + 1)], bank(0 + c % 2, 512, 0, 8),
                             AF.Identity, bias=routb[:]).then_inc(AS.h, 1)
            AS.c += 1
            rt_ev.append(AS.c)
        lgt_done = AS.c

        tr_mm = []
        for t_ in range(16):
            @T
            def _(t):
                t.wait_ge(AS.h, lgt_done)
                if t_ >= 2:
                    t.wait_ge(VS.h, ltok_cp[t_ - 2])
                t.transpose(bank(2 + t_ % 2, 8), LGT[:, 128 * t_:128 * (t_ + 1)],
                            ident[0:8, 0:8]).then_inc(TS.h, 1)
            TS.c += 1
            tr_mm.append(TS.c)
            if t_ == 0:
                ltok_cp = []

            @V
            def _(v):
                v.wait_ge(TS.h, tr_mm[t_])
                v.tensor_copy(LTOK[:, 8 * t_:8 * (t_ + 1)], bank(2 + t_ % 2, 8)
                              ).then_inc(VS.h, 1)
            VS.c += 1
            ltok_cp.append(VS.c)

        @V
        def _(v, snap=VS.c):
            v.wait_ge(VS.h, snap)
            v.tensor_copy(VMX, L3(0))
            v.memset(NDN, 1.0)
            v.memset(EID, 0.0)
            for e in range(1, 8):
                v.drain()
                v.tensor_tensor(VMX, VMX, L3(e), AOT.max)
            for e in range(8):
                v.drain()
                v.tensor_tensor(GEB, L3(e), VMX, AOT.is_ge)
                v.drain()
                v.tensor_tensor(PCK, GEB, NDN, AOT.mult)
                v.drain()
                v.tensor_copy(MK3(e), PCK)
                v.scalar_tensor_tensor(EID, PCK, float(e), EID, AOT.mult, AOT.add)
                v.drain()
                v.tensor_tensor(NDN, NDN, PCK, AOT.subtract)
            v.drain().then_inc(VS.h, 1)
        VS.c += 1
        amax_done = VS.c

        mkt_mm = []
        for t_ in range(16):
            @T
            def _(t):
                t.wait_ge(VS.h, amax_done)
                if t_ >= 2:
                    t.wait_ge(VS.h, mkt_cp[t_ - 2])
                t.transpose(bank(2 + t_ % 2, 128, 0, 8),
                            MK[:, 8 * t_:8 * (t_ + 1)], ident[:]).then_inc(TS.h, 1)
            TS.c += 1
            mkt_mm.append(TS.c)
            if t_ == 0:
                mkt_cp = []

            @V
            def _(v):
                v.wait_ge(TS.h, mkt_mm[t_])
                v.tensor_copy(MT[:, 128 * t_:128 * (t_ + 1)], bank(2 + t_ % 2, 128, 0, 8)
                              ).then_inc(VS.h, 1)
            VS.c += 1
            mkt_cp.append(VS.c)

        @V
        def _(v, snap=VS.c):
            v.wait_ge(VS.h, snap)
            v.tensor_tensor_scan(POS[:], MT[:], MT[:], 0.0, AOT.add, AOT.bypass
                                 ).then_inc(VS.h, 1)
        VS.c += 1
        pos_done = VS.c

        pt_mm = []
        for t_ in range(16):
            @T
            def _(t):
                t.wait_ge(VS.h, pos_done)
                if t_ >= 2:
                    t.wait_ge(VS.h, pt_cp[t_ - 2])
                t.transpose(bank(2 + t_ % 2, 8), POS[:, 128 * t_:128 * (t_ + 1)],
                            ident[0:8, 0:8]).then_inc(TS.h, 1)
            TS.c += 1
            pt_mm.append(TS.c)
            if t_ == 0:
                pt_cp = []

            @V
            def _(v):
                v.wait_ge(TS.h, pt_mm[t_])
                v.tensor_copy(PTOK[:, 8 * t_:8 * (t_ + 1)], bank(2 + t_ % 2, 8)
                              ).then_inc(VS.h, 1)
            VS.c += 1
            pt_cp.append(VS.c)

        @V
        def _(v, snap=VS.c):
            v.wait_ge(VS.h, snap)
            v.tensor_tensor(SLT, MK3(0), PT3(0), AOT.mult)
            for e in range(1, 8):
                v.drain()
                v.tensor_tensor(GEB, MK3(e), PT3(e), AOT.mult)
                v.drain()
                v.tensor_tensor(SLT, SLT, GEB, AOT.add)
            v.drain()
            v.tensor_scalar_sub(SLT, SLT, 1.0)
            v.drain()
            v.tensor_scalar_min(SLT, SLT, float(CAP - 1))
            v.drain()
            v.scalar_tensor_tensor(PFL, EID, float(CAP), SLT, AOT.mult, AOT.add)
            v.drain()
            v.tensor_copy(PIN, PFL)
            v.drain()
            v.tensor_single_scalar(T1I, PIN, 127, AOT.bitwise_and)
            v.drain()
            v.tensor_single_scalar(T2I, PIN, 7, AOT.arith_shift_right)
            v.drain()
            v.tensor_single_scalar(T1I, T1I, NCOLS, AOT.mult)
            v.drain()
            v.tensor_tensor(FLT.bitcast(I32), T1I, T2I, AOT.add)
            v.drain().then_inc(VS.h, 1)
        VS.c += 1
        flt_done = VS.c

        @G
        def _(g_):
            g_.iota(TKID, [[128, 16]], base=0, channel_multiplier=1)
            g_.iota(ILS, [[0, NCOLS]], base=2048, channel_multiplier=1
                    ).then_inc(GS.h, 1)
        GS.c += 1
        iota_done = GS.c

        @G
        def _(g_):
            g_.wait_ge(GS.h, iota_done)
            g_.dma_start(out=ap["ilist"].rearrange("(p c) one -> p (c one)", p=128),
                        in_=ILS).then_inc(GQ.h, 16)
            GQ.c += 16
            g_.wait_ge(GQ.h, GQ.c)
            g_.wait_ge(VS.h, flt_done)
            for t_ in range(16):
                g_.wait_ge(SCT.h, SCT.c)
                g_.indirect_dma_start(
                    out=ap["ilist"],
                    out_offset=IndirectOffsetOnAxis(FLT[:, t_:t_ + 1], 0),
                    in_=TKID[:, t_:t_ + 1], in_offset=None).then_inc(SCT.h, 16)
                SCT.c += 16
            g_.wait_ge(SCT.h, SCT.c)
            g_.dma_start(out=ILS,
                         in_=ap["ilist"].rearrange("(p c) one -> p (c one)", p=128)
                         ).then_inc(GQ.h, 16)
            GQ.c += 16
            g_.wait_ge(GQ.h, GQ.c)
            g_.nop().then_inc(GS.h, 1)
        GS.c += 1
        ilist_done = GS.c
        ILSB = ILS

        if debug:
            @S
            def _(s):
                s.wait_ge(GS.h, ilist_done)
                s.wait_ge(VS.h, flt_done)
                s.dma_start(out=ap["dbg_eid"], in_=EID).then_inc(LQ.h, 16)
                s.dma_start(out=ap["dbg_ilist"], in_=ILSB).then_inc(LQ.h, 16)
                LQ.c += 32

        if phases <= 9:
            _fin(S, G, LQ, GQ, AQ, GS, CQ, DSL, SSL, FIN, HS, GD, SCT)
            return nc

        def TTK(i): return pool[:, 16896 + 1024 * i:16896 + 1024 * (i + 1)]
        ttk_st = [HS[0].c, HS[1].c]
        stg_cp = []
        for t_ in range(16):
            tr_base = TS.c
            vbase_t = VS.c

            @T
            def _(t):
                if t_ == 0:
                    t.wait_ge(VS.h, xn1_done)
                for g in range(DG):
                    k = t_ * DG + g
                    if k >= 2:
                        if g >= 2:
                            t.wait_ge(VS.h, vbase_t + g - 1)
                        else:
                            t.wait_ge(VS.h, stg_cp[k - 2])
                    t.transpose(bank(4 + k % 2, 128), XN1(g)[:, 128 * t_:128 * (t_ + 1)],
                                ident[:]).then_inc(TS.h, 1)
                    TS.c += 1

            @V
            def _(v):
                for g in range(DG):
                    k = t_ * DG + g
                    v.wait_ge(TS.h, tr_base + g + 1)
                    if g == 0:
                        v.wait_ge(HS[t_ % 2].h, ttk_st[t_ % 2])
                    v.tensor_copy(TTK(t_ % 2)[:, 128 * g:128 * (g + 1)],
                                  bank(4 + k % 2, 128)).then_inc(VS.h, 1)
                    VS.c += 1
                    stg_cp.append(VS.c)

            @S
            def _(s, snap=VS.c):
                s.wait_ge(VS.h, snap)
                s.dma_start(out=ap["xn1tok"][128 * t_:128 * (t_ + 1), :],
                            in_=TTK(t_ % 2)[:]).then_inc(HS[t_ % 2].h, 16)
                HS[t_ % 2].c += 16
                ttk_st[t_ % 2] = HS[t_ % 2].c
        stage_vs = VS.c
        stage_ssl = (HS[0].c, HS[1].c)

        @V
        def _(v, snap=VS.c):
            v.wait_ge(VS.h, snap)
            v.wait_ge(HS[0].h, stage_ssl[0])
            v.wait_ge(HS[1].h, stage_ssl[1])
            v.memset(TTK(0)[:], 0.0)
            v.drain().then_inc(VS.h, 1)
        VS.c += 1
        zb_done = VS.c

        @S
        def _(s):
            s.wait_ge(VS.h, zb_done)
            s.dma_start(out=ap["xn1tok"][TOK:TOK + 128, :], in_=TTK(0)[:]
                        ).then_inc(LQ.h, 16)
            LQ.c += 16
            for t_ in range(17):
                s.dma_start(out=ap["moeout"][128 * t_:128 * (t_ + 1), :],
                            in_=TTK(0)[:]).then_inc(LQ.h, 16)
                LQ.c += 16
        zero_lq = LQ.c

        def VW(g): return pool[:, 33280 + 1024 * g:33280 + 1024 * (g + 1)]
        def KEV(i): return pool[:, 41472 + 2048 * i:41472 + 2048 * (i + 1)]
        def VTK(i): return pool[:, 42496 + 1024 * i:42496 + 1024 * (i + 1)]
        BVS = pool[:, 44544:45568]
        QW = [pool[:, 45568:46592], pool[:, 45568:46592]]
        BVR = pool[0:1, 41472:42496]

        qk_mm = []
        qk_ev = []
        qw_slot = [DSL[2].c, DSL[3].c]
        kev_st = [HS[0].c, HS[1].c]
        for o in range(16):
            @S
            def _(s):
                if o == 0:
                    s.wait_ge(VS.h, zb_done)
                if o >= 1:
                    s.wait_ge(TS.h, qk_mm[o - 1])
                s.dma_start(out=QW[0].rearrange("p (g c) -> p g c", g=8),
                            in_=ap["wqks"][o]).then_inc(DSL[2].h, 16)
                DSL[2].c += 16

            for c in range(4):
                @T
                def _(t):
                    if c == 0:
                        qw_slot[0] += 16
                        t.wait_ge(DSL[2].h, qw_slot[0])
                        t.wait_ge(VS.h, xn1_done)
                    if o * 4 + c >= 2:
                        t.wait_ge(AS.h, qk_ev[(o * 4 + c) - 2])
                    for g in range(DG):
                        mm = t.matmul(bank((o * 4 + c) % 2, 512),
                                      QW[0][:, 128 * g:128 * (g + 1)],
                                      XN1(g)[:, 512 * c:512 * (c + 1)],
                                      start=(g == 0), stop=(g == DG - 1))
                    mm.then_inc(TS.h, 1)
                TS.c += 1
                qk_mm.append(TS.c) if c == 3 else None
                if c == 3:
                    pass
                mm_snap = TS.c

                @A
                def _(a):
                    a.wait_ge(TS.h, mm_snap)
                    if o < 8:
                        dst = QT(o)[:, 512 * c:512 * (c + 1)]
                    else:
                        dst = KEV((o - 8) % 2)[:, 512 * c:512 * (c + 1)]
                    if o >= 8 and c == 0:
                        a.wait_ge(HS[(o - 8) % 2].h, kev_st[(o - 8) % 2])
                    if o < 8 and c == 0 and o < 2:
                        a.wait_ge(HS[0].h, stage_ssl[0])
                        a.wait_ge(HS[1].h, stage_ssl[1])
                        a.wait_ge(LQ.h, zero_lq)
                    a.activation(dst, bank((o * 4 + c) % 2, 512), AF.Identity,
                                 bias=consts[:, CBQKV + o:CBQKV + o + 1]
                                 ).then_inc(AS.h, 1)
                AS.c += 1
                qk_ev.append(AS.c)
            if len(qk_mm) <= o:
                qk_mm.append(TS.c)

            if o >= 8:
                @S
                def _(s, snap=AS.c):
                    s.wait_ge(AS.h, snap)
                    for cb in range(2):
                        s.dma_start(
                            out=ap["kag_in"][1024 * cb + 128 * (o - 8):
                                             1024 * cb + 128 * (o - 8) + 128, :],
                            in_=KEV((o - 8) % 2)[:, 1024 * cb:1024 * (cb + 1)]
                            ).then_inc(HS[(o - 8) % 2].h, 16)
                        HS[(o - 8) % 2].c += 16
                    kev_st[(o - 8) % 2] = HS[(o - 8) % 2].c
        qk_done_as = AS.c
        kag_ssl = (HS[0].c, HS[1].c)

        @S
        def _(s, snap=AS.c):
            s.wait_ge(AS.h, snap)
            for g in range(DG):
                s.dma_start(out=VW(g)[:],
                            in_=ap["wqkv"][128 * g:128 * (g + 1), 2048:3072]
                            ).then_inc(LQ.h, 16)
                LQ.c += 16
        vw_ld = LQ.c

        @S
        def _(s, snap=AS.c):
            s.wait_ge(AS.h, snap)
            s.wait_ge(HS[0].h, kag_ssl[0])
            s.wait_ge(HS[1].h, kag_ssl[1])
            s.dma_start(out=BVR, in_=ap["bv_row"]).then_inc(LQ.h, 16)
            LQ.c += 16
        bv_ld = LQ.c

        @T
        def _(t, snap=AS.c):
            t.wait_ge(LQ.h, bv_ld)
            t.wait_ge(AS.h, snap)
            t.matmul(bank(6, 512), ones1[:], BVR[:, 0:512], start=True,
                     stop=True)
            mm = t.matmul(bank(7, 512), ones1[:], BVR[:, 512:1024], start=True,
                          stop=True)
            mm.then_inc(TS.h, 1)
        TS.c += 1
        bv_rep = TS.c

        @V
        def _(v):
            v.wait_ge(TS.h, bv_rep)
            v.tensor_copy(BVS[:, 0:512], bank(6, 512))
            v.tensor_copy(BVS[:, 512:1024], bank(7, 512)).then_inc(VS.h, 1)
        VS.c += 1
        bv_sb = VS.c

        vt_mm = []
        vt_ev = []
        vt_st = [HS[0].c, HS[1].c]
        for t_ in range(16):
            @T
            def _(t):
                if t_ == 0:
                    t.wait_ge(VS.h, bv_sb)
                    t.wait_ge(LQ.h, bv_ld)
                if t_ >= 2:
                    t.wait_ge(VS.h, vt_ev[t_ - 2])
                for oc in range(2):
                    for g in range(DG):
                        mm = t.matmul(bank(oc + t_ % 2 * 2, 512),
                                      XN1(g)[:, 128 * t_:128 * (t_ + 1)],
                                      VW(g)[:, 512 * oc:512 * (oc + 1)],
                                      start=(g == 0), stop=(g == DG - 1))
                mm.then_inc(TS.h, 1)
            TS.c += 1
            vt_mm.append(TS.c)

            @V
            def _(v):
                v.wait_ge(TS.h, vt_mm[t_])
                if t_ >= 2:
                    v.wait_ge(HS[t_ % 2].h, vt_st[t_ % 2])
                for oc in range(2):
                    v.tensor_tensor(VTK(t_ % 2)[:, 512 * oc:512 * (oc + 1)],
                                    bank(oc + t_ % 2 * 2, 512),
                                    BVS[:, 512 * oc:512 * (oc + 1)], AOT.add
                                    ).then_inc(VS.h, 1)
                    VS.c += 1
            vt_ev.append(VS.c)

            @S
            def _(s, snap=VS.c):
                s.wait_ge(VS.h, snap)
                s.dma_start(out=ap["vag_in"][128 * t_:128 * (t_ + 1), :],
                            in_=VTK(t_ % 2)[:]).then_inc(HS[t_ % 2].h, 16)
                HS[t_ % 2].c += 16
                vt_st[t_ % 2] = HS[t_ % 2].c
        v_done_ssl = (HS[0].c, HS[1].c)

        @G
        def _(g_):
            g_.wait_ge(HS[0].h, v_done_ssl[0])
            g_.wait_ge(HS[1].h, v_done_ssl[1])
            g_.collective_compute(
                "AllGather", AOT.bypass, replica_groups=[list(range(NCORES))],
                ins=[ap["kag_in"].opt()], outs=[ap["kag_out"].opt()],
            ).then_inc(CC.h, 1)
            CC.c += 1
            g_.collective_compute(
                "AllGather", AOT.bypass, replica_groups=[list(range(NCORES))],
                ins=[ap["vag_in"].opt()], outs=[ap["vag_out"].opt()],
            ).then_inc(CC.h, 1)
            CC.c += 1
        kvag_done = CC.c

        if debug:
            @S
            def _(s):
                s.wait_ge(AS.h, qk_done_as)
                for g in range(DG):
                    s.dma_start(out=ap["dbg_q"][128 * g:128 * (g + 1), :],
                                in_=QT(g)[:]).then_inc(LQ.h, 16)
                    LQ.c += 16

        if phases <= 11:
            _fin(S, G, LQ, GQ, AQ, GS, CQ, DSL, SSL, FIN, HS, GD, SCT)
            return nc

        def XGT(g): return pool[:, 384 * g:384 * (g + 1)]
        def XGTOK(i): return pool[:, 3072 + 1024 * i:3072 + 1024 * (i + 1)]
        def WAb(i): return pool[:, 5120 + 2048 * i:5120 + 2048 * i + 1024]
        def WGb(i): return pool[:, 5120 + 2048 * i + 1024:5120 + 2048 * (i + 1)]
        def W2S(i): return pool[:, 9216 + 1024 * i:9216 + 1024 * (i + 1)]
        def SIGB(i): return pool[:, 11264 + 384 * i:11264 + 384 * (i + 1)]
        B1S = pool[:, 12032:12096]
        B2S = pool[:, 12096:12104]
        def YET(o): return pool[:, 12160 + 384 * o:12160 + 384 * (o + 1)]
        def YTOK(i): return pool[:, 3072 + 1024 * i:3072 + 1024 * (i + 1)]
        GATED = pool[:, 33280:45568]

        xg_slot = [GD[0].c, GD[1].c]
        w_slot = [DSL[2].c, DSL[3].c]
        yt_sc = [SCT.c, SCT.c]
        moe_tr = []
        moe_gate = []
        prev_w2mm = None
        for e in range(E):
            vb_e = VS.c

            @G
            def _(g_):
                g_.wait_ge(GS.h, ilist_done)
                if e == 0:
                    g_.wait_ge(VS.h, xn1_done)
                    g_.wait_ge(LQ.h, zero_lq)
                    g_.wait_ge(CC.h, kvag_done)
                else:
                    g_.wait_ge(SCT.h, SCT.c)
                for k in range(CAPT):
                    ek = e * CAPT + k
                    if k >= 2:
                        g_.wait_ge(VS.h, vb_e + 8 * (k - 1))
                    elif ek >= 2:
                        g_.wait_ge(VS.h, moe_tr[ek - 2])
                    g_.indirect_dma_start(
                        out=XGTOK(ek % 2)[:], out_offset=None,
                        in_=ap["xn1tok"],
                        in_offset=IndirectOffsetOnAxis(ILSB[:, ek:ek + 1], 0)
                        ).then_inc(GD[ek % 2].h, 16)
                    GD[ek % 2].c += 16

            for k in range(CAPT):
                ek = e * CAPT + k
                trb = TS.c

                vbk = VS.c

                @T
                def _(t):
                    xg_slot[ek % 2] += 16
                    t.wait_ge(GD[ek % 2].h, xg_slot[ek % 2])
                    if ek >= 2:
                        t.wait_ge(VS.h, moe_tr[ek - 2])
                    if e > 0 and k == 0:
                        t.wait_ge(AS.h, prev_yet_ev)
                    for g in range(DG):
                        if g >= 2:
                            t.wait_ge(VS.h, vbk + g - 1)
                        elif ek >= 1:
                            t.wait_ge(VS.h, moe_tr[ek - 1] - (1 - g))
                        t.transpose(bank(6 + g % 2, 128),
                                    XGTOK(ek % 2)[:, 128 * g:128 * (g + 1)],
                                    ident[:]).then_inc(TS.h, 1)
                        TS.c += 1

                @V
                def _(v):
                    for g in range(DG):
                        v.wait_ge(TS.h, trb + g + 1)
                        v.tensor_copy(XGT(g)[:, 128 * k:128 * (k + 1)],
                                      bank(6 + g % 2, 128)).then_inc(VS.h, 1)
                        VS.c += 1
                moe_tr.append(VS.c)
            xg_done = VS.c

            @S
            def _(s, esnap=xg_done):
                s.wait_ge(VS.h, esnap)
                s.dma_start(out=B1S, in_=ap["b1_r"][e]).then_inc(LQ.h, 16)
                s.dma_start(out=B2S, in_=ap["b2_r"][e]).then_inc(LQ.h, 16)
                LQ.c += 32
            b12_ld = LQ.c

            for pas in range(2):
                olo = 6 * pas
                nno = 6 if pas == 0 else 2
                for m in range(32):
                    em = (e * 2 + pas) * 32 + m

                    @S
                    def _(s):
                        if em >= 2:
                            s.wait_ge(TS.h, moe_w2mm[em - 2])
                        if pas == 0:
                            s.dma_start(
                                out=pool[:, 5120 + 2048 * (m % 2):
                                         5120 + 2048 * (m % 2) + 2048].rearrange(
                                    "p (g c) -> p g c", g=16),
                                in_=ap["w1c"][e, m]).then_inc(DSL[2 + m % 2].h, 16)
                            DSL[2 + m % 2].c += 16
                        if pas == 0:
                            s.dma_start(out=W2S(m % 2),
                                        in_=ap["w2"][e, 128 * m:128 * (m + 1), :]
                                        ).then_inc(DSL[2 + m % 2].h, 16)
                        else:
                            s.dma_start(out=W2S(m % 2)[:, 768:1024],
                                        in_=ap["w2"][e, 128 * m:128 * (m + 1),
                                                     768:1024]
                                        ).then_inc(DSL[2 + m % 2].h, 16)
                        DSL[2 + m % 2].c += 16

                    if em == 0:
                        moe_w2mm = []
                        moe_gate = []
                    w_slot[m % 2] = DSL[2 + m % 2].c
                    ge = e * 32 + m

                    if pas == 0:
                        @T
                        def _(t):
                            t.wait_ge(DSL[2 + m % 2].h, w_slot[m % 2])
                            if m == 0:
                                t.wait_ge(LQ.h, b12_ld)
                                t.wait_ge(VS.h, xg_done)
                            if ge >= 2:
                                t.wait_ge(VS.h, moe_gate[ge - 2])
                            for g in range(DG):
                                t.matmul(bank(0, 384), WAb(m % 2)[:, 128 * g:128 * (g + 1)],
                                         XGT(g)[:], start=(g == 0), stop=(g == DG - 1))
                            for g in range(DG):
                                mm = t.matmul(bank(1, 384),
                                              WGb(m % 2)[:, 128 * g:128 * (g + 1)],
                                              XGT(g)[:], start=(g == 0), stop=(g == DG - 1))
                            mm.then_inc(TS.h, 1)
                        TS.c += 1
                        w1_snap = TS.c

                        @A
                        def _(a):
                            a.wait_ge(TS.h, w1_snap)
                            if ge >= 2:
                                a.wait_ge(VS.h, moe_gate[ge - 2])
                            a.activation(SIGB(m % 2), bank(1, 384), AF.Sigmoid,
                                         bias=B1S[:, 32 + m:33 + m]).then_inc(AS.h, 1)
                        AS.c += 1
                        sig_snap = AS.c

                        @V
                        def _(v):
                            v.wait_ge(AS.h, sig_snap)
                            v.scalar_tensor_tensor(GATED[:, 384 * m:384 * (m + 1)],
                                                   bank(0, 384), B1S[:, m:m + 1],
                                                   SIGB(m % 2), AOT.add, AOT.mult
                                                   ).then_inc(VS.h, 1)
                        VS.c += 1
                        moe_gate.append(VS.c)

                    @T
                    def _(t):
                        if pas == 1:
                            t.wait_ge(DSL[2 + m % 2].h, w_slot[m % 2])
                        if pas == 0:
                            t.wait_ge(VS.h, moe_gate[ge])
                        elif m == 0:
                            t.wait_ge(VS.h, moe_gate[-1])
                        if m == 0 and pas == 1:
                            t.wait_ge(AS.h, yet_ev)
                        for o in range(nno):
                            mm = t.matmul(bank(2 + o, 384),
                                          W2S(m % 2)[:, 128 * (olo + o):128 * (olo + o + 1)],
                                          GATED[:, 384 * m:384 * (m + 1)],
                                          start=(m == 0), stop=(m == 31))
                        mm.then_inc(TS.h, 1)
                    TS.c += 1
                    moe_w2mm.append(TS.c)
                w2_snap = TS.c

                @A
                def _(a):
                    a.wait_ge(TS.h, w2_snap)
                    for o in range(nno):
                        a.activation(YET(olo + o), bank(2 + o, 384), AF.Identity,
                                     bias=B2S[:, olo + o:olo + o + 1])
                    a.drain().then_inc(AS.h, 1)
                AS.c += 1
                yet_ev = AS.c
            prev_yet_ev = AS.c

            for k in range(CAPT):
                ek2 = e * CAPT + k
                trb2 = TS.c

                if e == 0 and k == 0:
                    moe_ytr = []
                vbk2 = VS.c

                @T
                def _(t):
                    t.wait_ge(AS.h, prev_yet_ev)
                    if k == 0 and e > 0:
                        t.wait_ge(VS.h, moe_ytr[-1])
                    for o in range(DG):
                        if o >= 2:
                            t.wait_ge(VS.h, vbk2 + o - 1)
                        elif k >= 1:
                            t.wait_ge(VS.h, moe_ytr[-1] - (1 - o))
                        t.transpose(bank(6 + o % 2, 128),
                                    YET(o)[:, 128 * k:128 * (k + 1)],
                                    ident[:]).then_inc(TS.h, 1)
                        TS.c += 1

                @V
                def _(v):
                    for o in range(DG):
                        v.wait_ge(TS.h, trb2 + o + 1)
                        if o == 0:
                            v.wait_ge(SCT.h, yt_sc[ek2 % 2])
                        v.tensor_copy(YTOK(ek2 % 2)[:, 128 * o:128 * (o + 1)],
                                      bank(6 + o % 2, 128)).then_inc(VS.h, 1)
                        VS.c += 1
                moe_ytr.append(VS.c)

                @G
                def _(g_, snap=VS.c):
                    g_.wait_ge(VS.h, snap)
                    g_.wait_ge(SCT.h, SCT.c)
                    g_.indirect_dma_start(
                        out=ap["moeout"],
                        out_offset=IndirectOffsetOnAxis(ILSB[:, ek2:ek2 + 1], 0),
                        in_=YTOK(ek2 % 2)[:], in_offset=None
                        ).then_inc(SCT.h, 16)
                SCT.c += 16
                yt_sc[ek2 % 2] = SCT.c
        moe_ssl = (SCT, SCT.c)
        moe_vs = VS.c

        if debug:
            @S
            def _(s):
                s.wait_ge(SCT.h, moe_ssl[1])
                s.dma_start(out=ap["dbg_moe"], in_=ap["moeout"]).then_inc(LQ.h, 16)
                LQ.c += 16

        if phases <= 12:
            _fin(S, G, LQ, GQ, AQ, GS, CQ, DSL, SSL, FIN, HS, GD, SCT)
            return nc

        def VQb(i): return pool[:, 8192 * i:8192 * (i + 1)]
        VIDX = pool[:, 16384:16416].bitcast(U32)
        KIDX = pool[:, 16416:16448].bitcast(U32)
        def KH(i): return pool[:, 33280 + 1024 * i:33280 + 1024 * (i + 1)]
        L0row = pool[0:1, 35328:37376]
        def PHA(i): return pool[:, 37376 + 512 * i:37376 + 512 * (i + 1)]
        def PHBb(i): return pool[:, 38400 + 512 * i:38400 + 512 * (i + 1)]
        VA16 = pool[:, 39424:40464]
        def OACC(i): return pool[0:65, 43520 + 2048 * i:43520 + 2048 * (i + 1)]
        ONPB = pool[:, 41472:43520]
        ONP = QT
        CQSC = 74

        @S
        def _(s, snap=moe_vs):
            s.wait_ge(VS.h, snap)
            s.dma_start(out=VIDX, in_=ap["vqidx"]).then_inc(LQ.h, 16)
            s.dma_start(out=KIDX, in_=ap["kidx"]).then_inc(LQ.h, 16)
            LQ.c += 32
        idx_ld = LQ.c

        @G
        def _(g_):
            g_.wait_ge(VS.h, moe_vs)
            g_.wait_ge(CC.h, kvag_done)
            g_.wait_ge(SCT.h, moe_ssl[1])
            g_.memset(consts[:, CQSC:CQSC + 1], 0.125)
            for j in range(16):
                g_.memset(VA16[:, 65 * j + 64:65 * j + 65], 1.0)
            g_.nop().then_inc(GS.h, 1)
        GS.c += 1
        att_init = GS.c

        vq_slot = [GD[0].c, GD[1].c]
        kh_slot = [GD[2].c, GD[3].c]
        onp_done = []
        for hp in range(8):
            for q in range(4):
                @G
                def _(g_):
                    if hp == 0 and q == 0:
                        g_.wait_ge(GS.h, att_init)
                        g_.wait_ge(LQ.h, idx_ld)
                    if hp > 0 or q >= 2:
                        g_.wait_ge(TS.h, vaq_guard[(hp * 4 + q) - 2])
                    for j in range(DG):
                        g_.indirect_dma_start(
                            out=VQb(q % 2)[:, 1024 * j:1024 * (j + 1)],
                            out_offset=None, in_=ap["vag_out"],
                            in_offset=IndirectOffsetOnAxis(
                                VIDX[:, 8 * q + j:8 * q + j + 1], 0)
                            ).then_inc(GD[q % 2].h, 16)
                        GD[q % 2].c += 16
                    g_.indirect_dma_start(
                        out=KH(q % 2)[:], out_offset=None,
                        in_=ap["kag_out"],
                        in_offset=IndirectOffsetOnAxis(
                            KIDX[:, 4 * hp + q:4 * hp + q + 1], 0)
                        ).then_inc(GD[2 + q % 2].h, 16)
                    GD[2 + q % 2].c += 16
                if hp == 0 and q == 0:
                    vaq_guard = []
                vq_slot[q % 2] = GD[q % 2].c
                kh_slot[q % 2] = GD[2 + q % 2].c

                @V
                def _(v):
                    v.wait_ge(GD[q % 2].h, vq_slot[q % 2])
                    if hp + q > 0:
                        v.wait_ge(TS.h, pv_last)
                    for i in range(2):
                        hh = 2 * hp + i
                        for j in range(DG):
                            v.tensor_copy(VA16[:, 65 * (8 * i + j):65 * (8 * i + j) + 64],
                                          VQb(q % 2)[:, 1024 * j + 64 * hh:
                                                     1024 * j + 64 * hh + 64])
                    v.drain().then_inc(VS.h, 1)
                VS.c += 1
                va_cp = VS.c

                for qc in range(4):
                    exp_as = []
                    st_mm = []
                    pv_mm = []

                    def emit_S(j, qc=qc):
                        @T
                        def _(t):
                            if j == 0 and qc == 0:
                                t.wait_ge(GD[2 + q % 2].h, kh_slot[q % 2])
                                t.wait_ge(VS.h, va_cp)
                                if q == 0 and hp > 0:
                                    t.wait_ge(VS.h, onp_done[-1])
                            if j >= 2:
                                t.wait_ge(AS.h, exp_as[2 * (j - 2)])
                                t.wait_ge(AS.h, exp_as[2 * (j - 2) + 1])
                            t.matmul(bank(0 + 4 * (j % 2), 512),
                                     KH(q % 2)[0:64, 128 * j:128 * (j + 1)],
                                     QT(hp)[0:64, 512 * qc:512 * (qc + 1)],
                                     start=True, stop=True,
                                     tile_position=(0, 0)).then_inc(TS.h, 1)
                            TS.c += 1
                            t.matmul(bank(1 + 4 * (j % 2), 512),
                                     KH(q % 2)[64:128, 128 * j:128 * (j + 1)],
                                     QT(hp)[64:128, 512 * qc:512 * (qc + 1)],
                                     start=True, stop=True,
                                     tile_position=(64, 0)).then_inc(TS.h, 1)
                            TS.c += 1
                            st_mm.append(TS.c)

                    def emit_E(j):
                        @A
                        def _(a):
                            a.wait_ge(TS.h, st_mm[j])
                            if j >= 2:
                                a.wait_ge(TS.h, pv_mm[2 * (j - 2) + 1])
                            a.activation(PHA(j % 2), bank(0 + 4 * (j % 2), 512),
                                         AF.Exp, scale=consts[:, CQSC:CQSC + 1]
                                         ).then_inc(AS.h, 1)
                            AS.c += 1
                            exp_as.append(AS.c)
                            a.activation(PHBb(j % 2), bank(1 + 4 * (j % 2), 512),
                                         AF.Exp, scale=consts[:, CQSC:CQSC + 1]
                                         ).then_inc(AS.h, 1)
                            AS.c += 1
                            exp_as.append(AS.c)

                    def emit_PV(j, qc=qc):
                        @T
                        def _(t):
                            t.wait_ge(AS.h, exp_as[2 * j + 1])
                            if j == 0 and (hp + q + qc > 0):
                                t.wait_ge(VS.h, oev_last)
                            t.matmul(bank(2, 512, 0, 65),
                                     VA16[:, 65 * j:65 * j + 65],
                                     PHA(j % 2),
                                     start=(j == 0), stop=(j == DG - 1)
                                     ).then_inc(TS.h, 1)
                            TS.c += 1
                            t.matmul(bank(3, 512, 0, 65),
                                     VA16[:, 65 * (8 + j):65 * (8 + j) + 65],
                                     PHBb(j % 2),
                                     start=(j == 0), stop=(j == DG - 1)
                                     ).then_inc(TS.h, 1)
                            TS.c += 1
                            pv_mm.append(TS.c - 1)
                            pv_mm.append(TS.c)

                    for j in range(DG):
                        emit_S(j)
                        emit_E(j)
                        if j >= 1:
                            emit_PV(j - 1)
                    emit_PV(DG - 1)
                    pv_last = TS.c

                    @V
                    def _(v, qc=qc):
                        v.wait_ge(TS.h, pv_last)
                        v.drain()
                        for i in range(2):
                            if q == 0:
                                v.tensor_copy(OACC(i)[:, 512 * qc:512 * (qc + 1)],
                                              bank(2 + i, 512, 0, 65))
                            else:
                                v.tensor_tensor(OACC(i)[:, 512 * qc:512 * (qc + 1)],
                                                OACC(i)[:, 512 * qc:512 * (qc + 1)],
                                                bank(2 + i, 512, 0, 65), AOT.add)
                        v.drain().then_inc(VS.h, 1)
                    VS.c += 1
                    oev_last = VS.c
                vaq_guard.append(TS.c)

            for i in range(2):
                @V
                def _(v, snap=VS.c):
                    v.wait_ge(VS.h, snap)
                    v.drain()
                    v.reciprocal(OACC(i)[64:65, :], OACC(i)[64:65, :]
                                 ).then_inc(VS.h, 1)
                VS.c += 1
                rec_done = VS.c

                @G
                def _(g_):
                    g_.wait_ge(VS.h, rec_done)
                    g_.dma_start(out=L0row, in_=OACC(i)[64:65, :]).then_inc(GQ.h, 16)
                    GQ.c += 16
                l0_ld = GQ.c

                norm_vs = []
                for qc in range(4):
                    @T
                    def _(t):
                        t.wait_ge(GQ.h, l0_ld)
                        if qc >= 1:
                            t.wait_ge(VS.h, norm_vs[qc - 1])
                        t.matmul(bank(6, 512, 0, 64), ones1[0:1, 0:64],
                                 L0row[:, 512 * qc:512 * (qc + 1)],
                                 start=True, stop=True).then_inc(TS.h, 1)
                    TS.c += 1
                    lr_mm = TS.c

                    @V
                    def _(v):
                        v.wait_ge(TS.h, lr_mm)
                        dst = (ONP(hp)[0:64, 512 * qc:512 * (qc + 1)] if i == 0
                               else ONPB[0:64, 512 * qc:512 * (qc + 1)])
                        v.tensor_tensor(dst, OACC(i)[0:64, 512 * qc:512 * (qc + 1)],
                                        bank(6, 512, 0, 64), AOT.mult
                                        ).then_inc(VS.h, 1)
                    VS.c += 1
                    norm_vs.append(VS.c)

                if i == 1:
                    @G
                    def _(g_, snap=VS.c):
                        g_.wait_ge(VS.h, snap)
                        g_.dma_start(out=ONP(hp)[64:128, :], in_=ONPB[0:64, :]
                                     ).then_inc(GQ.h, 16)
                        GQ.c += 16
                    onp_gq = GQ.c

                    @V
                    def _(v):
                        v.wait_ge(GQ.h, onp_gq)
                        v.engine_nop().then_inc(VS.h, 1)
                    VS.c += 1
            onp_done.append(VS.c)
        att_done_vs = VS.c

        def WOT(g): return pool[:, 1024 * g:1024 * (g + 1)]
        EVB = [pool[:, 8192 + 512 * i:8192 + 512 * (i + 1)] for i in range(2)]

        @S
        def _(s, snap=att_done_vs):
            s.wait_ge(VS.h, snap)
            for g in range(DG):
                s.dma_start(out=WOT(g)[:], in_=ap["wo"][128 * g:128 * (g + 1), :]
                            ).then_inc(LQ.h, 16)
                LQ.c += 16
        wo_ld = LQ.c

        op2_ssl = [SSL[0].c, SSL[1].c]
        for o in range(DG):
            for qc in range(4):
                oq = o * 4 + qc

                @T
                def _(t):
                    if oq == 0:
                        t.wait_ge(LQ.h, wo_ld)
                        t.wait_ge(VS.h, att_done_vs)
                    if oq >= 2:
                        t.wait_ge(AS.h, op2_ev[oq - 2])
                    for gp in range(DG):
                        mm = t.matmul(bank(oq % 2, 512), WOT(gp)[:, 128 * o:128 * (o + 1)],
                                      ONP(gp)[:, 512 * qc:512 * (qc + 1)],
                                      start=(gp == 0), stop=(gp == DG - 1))
                    mm.then_inc(TS.h, 1)
                TS.c += 1
                op2_mm = TS.c
                if oq == 0:
                    op2_ev = []

                @A
                def _(a):
                    a.wait_ge(TS.h, op2_mm)
                    if oq >= 2:
                        a.wait_ge(SSL[oq % 2].h, op2_ssl[oq % 2])
                    a.activation(EVB[oq % 2], bank(oq % 2, 512), AF.Identity,
                                 bias=consts[:, CBO + o:CBO + o + 1]).then_inc(AS.h, 1)
                AS.c += 1
                op2_ev.append(AS.c)

                @G
                def _(g_, snap=AS.c):
                    g_.wait_ge(AS.h, snap)
                    g_.dma_start(out=ap["xres"][128 * o:128 * (o + 1),
                                                OFF + 512 * qc:OFF + 512 * (qc + 1)],
                                 in_=EVB[oq % 2], accum_op=AOT.add
                                 ).then_inc(SSL[oq % 2].h, 16)
                SSL[oq % 2].c += 16
                op2_ssl[oq % 2] = SSL[oq % 2].c
        att_add_ssl = (SSL[0].c, SSL[1].c)

        if debug:
            @S
            def _(s):
                s.wait_ge(SSL[0].h, att_add_ssl[0])
                s.wait_ge(SSL[1].h, att_add_ssl[1])
                s.dma_start(out=ap["dbg_att"], in_=ap["xres"]).then_inc(LQ.h, 16)
                LQ.c += 16

        def XR2(g): return pool[:, 16896 + 2048 * g:16896 + 2048 * (g + 1)]
        def MOT(i): return pool[:, 1024 * i:1024 * (i + 1)]
        def OB(i): return pool[:, 2048 + 1024 * i:2048 + 1024 * (i + 1)]

        @S
        def _(s):
            s.wait_ge(SSL[0].h, att_add_ssl[0])
            s.wait_ge(SSL[1].h, att_add_ssl[1])
            s.wait_ge(TS.h, TS.c)
            for g in range(DG):
                s.dma_start(out=XR2(g)[:], in_=ap["xres"][128 * g:128 * (g + 1), OFF:]
                            ).then_inc(LQ.h, 16)
                LQ.c += 16
        xr2_ld = LQ.c

        mot_slot = [DSL[0].c, DSL[1].c]
        ob_st = [HS[0].c, HS[1].c]
        fin_cp = []
        for t_ in range(16):
            @S
            def _(s):
                if t_ == 0:
                    s.wait_ge(SCT.h, moe_ssl[1])
                if t_ >= 2:
                    s.wait_ge(VS.h, fin_cp[t_ - 2])
                s.dma_start(out=MOT(t_ % 2)[:],
                            in_=ap["moeout"][128 * t_:128 * (t_ + 1), :]
                            ).then_inc(DSL[t_ % 2].h, 16)
                DSL[t_ % 2].c += 16

            trf = TS.c

            vbf = VS.c

            @T
            def _(t):
                if t_ == 0:
                    t.wait_ge(LQ.h, xr2_ld)
                if t_ >= 2:
                    t.wait_ge(VS.h, fin_cp[t_ - 2])
                for g in range(DG):
                    if g >= 2:
                        t.wait_ge(VS.h, vbf + g - 1)
                    elif t_ >= 1:
                        t.wait_ge(VS.h, fin_cp[-1] - (1 - g))
                    t.transpose(bank(g % 2, 128), XR2(g)[:, 128 * t_:128 * (t_ + 1)],
                                ident[:]).then_inc(TS.h, 1)
                    TS.c += 1

            @V
            def _(v):
                mot_slot[t_ % 2] += 16
                v.wait_ge(DSL[t_ % 2].h, mot_slot[t_ % 2])
                if t_ >= 2:
                    v.wait_ge(HS[t_ % 2].h, ob_st[t_ % 2])
                for g in range(DG):
                    v.wait_ge(TS.h, trf + g + 1)
                    v.tensor_tensor(OB(t_ % 2)[:, 128 * g:128 * (g + 1)],
                                    bank(g % 2, 128),
                                    MOT(t_ % 2)[:, 128 * g:128 * (g + 1)], AOT.add
                                    ).then_inc(VS.h, 1)
                    VS.c += 1
            fin_cp.append(VS.c)

            @S
            def _(s, snap=VS.c):
                s.wait_ge(VS.h, snap)
                s.dma_start(out=ap["out"][128 * t_:128 * (t_ + 1), :],
                            in_=OB(t_ % 2)[:]).then_inc(HS[t_ % 2].h, 16)
                HS[t_ % 2].c += 16
                ob_st[t_ % 2] = HS[t_ % 2].c

        _fin(S, G, LQ, GQ, AQ, GS, CQ, DSL, SSL, FIN, HS, GD, SCT)
    return nc


def _fin(S, G, LQ, GQ, AQ, GS, CQ=None, DSL=(), SSL=(), FIN=None, HS=(), GD=(),
         SCT=None):
    @G
    def _(g):
        g.wait_ge(GQ.h, GQ.c)
        for x in SSL:
            g.wait_ge(x.h, x.c)
        for x in GD:
            g.wait_ge(x.h, x.c)
        if SCT is not None:
            g.wait_ge(SCT.h, SCT.c)
        g.nop().then_inc(FIN.h, 1)
    FIN.c += 1

    @S
    def _(s):
        s.wait_ge(LQ.h, LQ.c)
        s.wait_ge(AQ.h, AQ.c)
        if CQ is not None:
            s.wait_ge(CQ.h, CQ.c)
        for x in DSL:
            s.wait_ge(x.h, x.c)
        for x in HS:
            s.wait_ge(x.h, x.c)
        s.wait_ge(FIN.h, FIN.c)



def make_inmaps(inp):
    x = np.ascontiguousarray(np.asarray(inp["L0_attn_x"], np.float32))
    ident = np.eye(128, dtype=np.float32)
    eyerep = np.zeros((16, 16 * 128), np.float32)
    for n in range(16):
        eyerep[n, 128 * n:128 * (n + 1)] = 1.0
    alog = np.asarray(inp["L0_attn_A_log"], np.float32)
    alog_r = np.zeros((128, DG * NST), np.float32)
    for g in range(DG):
        alog_r[:, NST * g:NST * (g + 1)] = alog[128 * g:128 * (g + 1), :]

    def colsplit(v):
        return np.ascontiguousarray(np.asarray(v, np.float32).reshape(DG, 128).T)

    bqkv = np.asarray(inp["L1_attn_b_qkv"], np.float32)

    def slice_lhsT(wT):
        k_in, n_out = wT.shape
        return np.ascontiguousarray(
            wT.reshape(k_in // 128, 128, n_out // 128, 128).transpose(2, 1, 0, 3))

    wrT = np.asarray(inp["L0_ffn_W_r"], np.float32).T
    wkT = np.asarray(inp["L0_ffn_W_k"], np.float32).T
    wrk = slice_lhsT(np.ascontiguousarray(wrT))
    wkk = slice_lhsT(np.ascontiguousarray(wkT))
    wrks = np.ascontiguousarray(np.concatenate([wrk, wkk], axis=2))
    wvs = slice_lhsT(np.ascontiguousarray(np.asarray(inp["L0_ffn_W_v"], np.float32).T))
    wouts = slice_lhsT(np.ascontiguousarray(
        np.asarray(inp["L0_attn_W_out"], np.float32).T))
    wqks = slice_lhsT(np.ascontiguousarray(
        np.asarray(inp["L1_attn_W_qkv"], np.float32)[:, 0:2048]))
    w1 = np.asarray(inp["L1_ffn_w1"], np.float32)
    w1a = w1[:, :, 0:4096].reshape(E, 8, 128, 32, 128).transpose(0, 3, 2, 1, 4)
    w1g = w1[:, :, 4096:8192].reshape(E, 8, 128, 32, 128).transpose(0, 3, 2, 1, 4)
    w1c = np.ascontiguousarray(np.concatenate([w1a, w1g], axis=3))
    com = {
        "ident": ident, "eyerep": eyerep, "alog_r": alog_r,
        "wdt_r": colsplit(inp["L0_attn_W_dt"][:, 0]),
        "bdt_r": colsplit(inp["L0_attn_b_dt"]),
        "mur_r": colsplit(inp["L0_ffn_mu_r"]),
        "muk_r": colsplit(inp["L0_ffn_mu_k"]),
        "bout_r": colsplit(inp["L0_attn_b_out"]),
        "bo_r": colsplit(inp["L1_attn_b_o"]),
        "bqkv_r": np.ascontiguousarray(bqkv[:2048].reshape(16, 128).T),
        "bv_row": np.ascontiguousarray(bqkv[2048:].reshape(1, 1024)),
        "wxpT": np.ascontiguousarray(np.asarray(inp["L0_attn_W_xp"], np.float32).T),
        "wrks": wrks, "wvs": wvs, "wouts": wouts, "wqks": wqks, "w1c": w1c,
        "bxp": np.ascontiguousarray(
            np.asarray(inp["L0_attn_b_xp"], np.float32).reshape(2, 16).T),
        "wqkv": np.ascontiguousarray(np.asarray(inp["L1_attn_W_qkv"], np.float32)),
        "wo": np.ascontiguousarray(np.asarray(inp["L1_attn_W_o"], np.float32)),
        "routw": np.ascontiguousarray(np.asarray(inp["L1_ffn_router_w"], np.float32)),
        "routb": np.asarray(inp["L1_ffn_router_b"], np.float32).reshape(E, 1),
        "b1_r": np.ascontiguousarray(
            np.asarray(inp["L1_ffn_b1"], np.float32).reshape(E, 64, 128
                                                             ).transpose(0, 2, 1)),
        "w2": np.ascontiguousarray(np.asarray(inp["L1_ffn_w2"], np.float32)),
        "b2_r": np.ascontiguousarray(
            np.asarray(inp["L1_ffn_b2"], np.float32).reshape(E, DG, 128
                                                             ).transpose(0, 2, 1)),
    }

    maps = []
    prng = np.arange(128, dtype=np.uint32)
    for c in range(NCORES):
        b, h = c // 2, c % 2
        start = h * TOK
        xT = np.zeros((D, NTOT), np.float32)
        xT[:, OFF:] = x[b, start:start + TOK].T
        if start > 0:
            xT[:, OFF - 1] = x[b, start - 1]
        selp = np.zeros((128, NCORES), np.float32)
        if h == 1:
            selp[:, c - 1] = 1.0
        kidx = np.zeros((128, 32), np.uint32)
        for hp in range(8):
            for q in range(4):
                base = (2 * b + q // 2) * 2048 + (q % 2) * 1024
                kidx[:64, hp * 4 + q] = base + 64 * (2 * hp) + prng[:64]
                kidx[64:, hp * 4 + q] = base + 64 * (2 * hp + 1) + prng[:64]
        vqidx = np.zeros((128, 32), np.uint32)
        for q in range(4):
            for j in range(8):
                base = (2 * b + q // 2) * 2048 + (q % 2) * 1024 + 128 * j
                vqidx[:, q * 8 + j] = base + prng
        m = dict(com)
        m.update({"xT": xT, "selp": selp, "kidx": kidx, "vqidx": vqidx})
        maps.append(m)
    return maps


def kernel(**inputs):
    nc = build_nc(debug=False)
    maps = make_inmaps(inputs)
    res = run_bass_kernel_spmd(nc, maps, core_ids=list(range(NCORES)))
    out = np.zeros((4, 4096, 1024), np.float32)
    for c in range(NCORES):
        b, h = c // 2, c % 2
        out[b, h * TOK:(h + 1) * TOK, :] = res.results[c]["out"]
    return out
